# version 17 (final)
# baseline (speedup 1.0000x reference)
import numpy as np
from contextlib import ExitStack

import concourse.bass as bass
import concourse.mybir as mybir
from concourse.bass_utils import run_bass_kernel_spmd

F32 = mybir.dt.float32
BF16 = mybir.dt.bfloat16
I32 = mybir.dt.int32
U8 = mybir.dt.uint8
AF = mybir.ActivationFunctionType
ALU = mybir.AluOpType

C = 4096
D = 1024
T = 1024
NFC = C // 128
OSL = 512
EPS = 1e-5
AQT = 49
PQT = 104


def emit_bits_full(v, qwt_ap, b0u8, b0p, bup):
    li = None
    for p in range(8):
        src = qwt_ap[:, p, :].bitcast(U8)
        for s in range(8):
            dst = b0u8[:, s::8] if p == 0 else bup[:, p - 1, s::8]
            li = v.tensor_scalar(dst, src, s, 1, ALU.logical_shift_right,
                                 ALU.bitwise_and)
    li = v.tensor_copy(b0p[:], b0u8[:])
    return li


def plan_trees():
    st = {'lf': 0, 'mg': 0, 'rd': {}, 'use': {lv: 0 for lv in range(1, 8)}}
    lf8 = {k: [] for k in range(4)}
    lf4 = {k: [] for k in range(4)}
    mg8 = {k: [] for k in range(4)}
    mg4 = {k: [] for k in range(4)}
    mg8_end, mg4_end = {}, {}

    def rec(nbits, base, did, cc, tt, lfl, mgl):
        if nbits == 1:
            st['lf'] += 1
            lfl.append((did, cc, base, st['rd'].get(did, 0), tt))
            return
        lv = nbits - 1
        par = st['use'][lv] % 2 if lv <= 3 else 0
        st['use'][lv] += 1
        sid = ('s', lv, par)
        half = 1 << lv
        rec(lv, base + half, sid, cc, tt, lfl, mgl)
        rec(lv, base, did, cc, tt, lfl, mgl)
        st['mg'] += 1
        mgl.append((did, lv, par, cc, st['lf']))
        st['rd'][sid] = st['mg']
        st['rd'][did] = st['mg']

    mg8_mid, mg4_mid = {}, {}
    for k in range(4):
        rec(8, 0, ('w8', 0), 0, 8, lf8[k], mg8[k])
        mg8_mid[k] = st['mg']
        rec(8, 0, ('w8', 1), 1, 8, lf8[k], mg8[k])
        mg8_end[k] = st['mg']
        rec(4, 0, ('w4', 0), 0, 4, lf4[k], mg4[k])
        mg4_mid[k] = st['mg']
        rec(4, 0, ('w4', 1), 1, 4, lf4[k], mg4[k])
        mg4_end[k] = st['mg']
    return lf8, lf4, mg8, mg4, mg8_mid, mg8_end, mg4_mid, mg4_end


def build_program(debug=False):
    nc = bass.Bass("TRN2", target_bir_lowering=False)

    xT = nc.dram_tensor("xT", [C, T], F32, kind="ExternalInput")
    wqkv = nc.dram_tensor("wqkv", [C, 3 * D], F32, kind="ExternalInput")
    bqkv_c = nc.dram_tensor("bqkv_c", [128, 24], F32, kind="ExternalInput")
    wo = nc.dram_tensor("wo", [D, C], F32, kind="ExternalInput")
    bo_c = nc.dram_tensor("bo_c", [128, 32], F32, kind="ExternalInput")
    qw_s = nc.dram_tensor("qw_s", [8, OSL, 128], I32, kind="ExternalInput")
    l4_s = nc.dram_tensor("l4_s", [OSL, 16], F32, kind="ExternalInput")
    l8_s = nc.dram_tensor("l8_s", [OSL, 256], F32, kind="ExternalInput")
    fb_c = nc.dram_tensor("fb_c", [128, 32], F32, kind="ExternalInput")
    identf = nc.dram_tensor("identf", [128, 128], F32, kind="ExternalInput")
    identb = nc.dram_tensor("identb", [128, 128], BF16, kind="ExternalInput")
    onecol_b = nc.dram_tensor("onecol_b", [128, 1], BF16, kind="ExternalInput")
    onerow_f = nc.dram_tensor("onerow_f", [1, 128], F32, kind="ExternalInput")

    xout = nc.dram_tensor("xout", [C, T], F32, kind="ExternalOutput")

    k_loc = nc.dram_tensor("k_loc", [D, T], BF16)
    v_loc = nc.dram_tensor("v_loc", [T, D], BF16)
    k_full = nc.dram_tensor("k_full", [4 * D, T], BF16)
    v_full = nc.dram_tensor("v_full", [4 * T, D], BF16)
    wl_loc = nc.dram_tensor("wl_loc", [4, C, 128], BF16)
    wh_loc = nc.dram_tensor("wh_loc", [4, C, 128], BF16)
    wl_full = nc.dram_tensor("wl_full", [4, 8 * C, 128], BF16)
    wh_full = nc.dram_tensor("wh_full", [4, 8 * C, 128], BF16)
    x2d = nc.dram_tensor("x2d", [C, T], F32)
    rowd = nc.dram_tensor("rowd", [3, T], F32)
    dbg = {}
    if debug:
        for nm, shp, dt_ in [("d_rbc", [128, T], F32), ("d_mrbc", [128, T], F32),
                             ("d_xln", [128, T], BF16), ("d_q", [128, T], BF16),
                             ("d_kloc", [128, T], BF16), ("d_vloc", [128, D], BF16),
                             ("d_kfull", [128, T], BF16), ("d_x2", [128, T], F32),
                             ("d_mk", [128, T], F32), ("d_wl", [128, 1024], BF16),
                             ("d_wh", [128, 1024], BF16), ("d_xln2", [128, T], BF16),
                             ("d_probs", [128, 4096], BF16), ("d_ao", [128, T], BF16)]:
            dbg[nm] = nc.dram_tensor(nm, shp, dt_, kind="ExternalOutput")

    with ExitStack() as stk:
        E = stk.enter_context
        xtile = E(nc.sbuf_tensor("xtile", [128, 2, T], F32))
        tmpf = E(nc.sbuf_tensor("tmpf", [128, T], F32))
        xcb = E(nc.sbuf_tensor("xcb", [128, 2, T], BF16))
        xsq = E(nc.sbuf_tensor("xsq", [128, 2, T], BF16))
        xln = E(nc.sbuf_tensor("xln", [128, NFC, T], BF16))
        kst = xln[:, 0:16, :].rearrange("p (a c) n -> p a c n", a=2)
        vst = xln[:, 16:32, :].rearrange("p (a c) n -> p a c n", a=2)
        pmax = E(nc.sbuf_tensor("pmax", [128, 8], F32))
        dnp = E(nc.sbuf_tensor("dnp", [128, 8], F32))
        rcol = E(nc.sbuf_tensor("rcol", [128, 2], F32))
        wstr = E(nc.sbuf_tensor("wstr", [128, 2, 512], F32))
        wstb = E(nc.sbuf_tensor("wstb", [128, 2, 512], BF16))
        rbc = E(nc.sbuf_tensor("rbc", [128, T], F32))
        mrbc = E(nc.sbuf_tensor("mrbc", [128, T], F32))
        mkbc = E(nc.sbuf_tensor("mkbc", [128, T], F32))
        bqv = E(nc.sbuf_tensor("bqv", [128, 24], F32))
        bob = E(nc.sbuf_tensor("bob", [128, 32], F32))
        fbb = E(nc.sbuf_tensor("fbb", [128, 32], F32))
        idf = E(nc.sbuf_tensor("idf", [128, 128], F32))
        idb = E(nc.sbuf_tensor("idb", [128, 128], BF16))
        onec = E(nc.sbuf_tensor("onec", [128, 1], BF16))
        oner = E(nc.sbuf_tensor("oner", [1, 128], F32))
        evtf = E(nc.sbuf_tensor("evtf", [128, 8, 128], F32))
        wts = E(nc.sbuf_tensor("wts", [128, 4, 128], BF16))
        shA = E(nc.sbuf_tensor("shA", [128, 90496], U8, align_bytes=128))

        def carve(off, nbytes, dt, shape=None):
            ap = shA[:, off:off + nbytes].bitcast(dt)
            if shape is not None:
                ap = ap.rearrange("p (a b) -> p a b", a=shape[0])
            return ap

        q_sb = carve(0, 16384, BF16, (8, T))
        kstg = carve(16384, 16384, BF16, (8, T))
        ssb = carve(16384, 16384, F32, (8, 512))
        ssbf = carve(16384, 16384, F32)
        vstage = carve(32768, 16384, BF16, (8, T))
        probs = carve(49152, 8192, BF16)
        probs2 = carve(32768, 8192, BF16)
        probsT = carve(57344, 8192, BF16, (32, 128))
        aoT = carve(65536, 16384, BF16, (8, T))
        aostg = carve(81920, 2048, BF16)
        vtile = carve(83968, 2048, BF16, (8, 128))
        rows = shA[0:1, 0:24576].bitcast(F32).rearrange("p (a b) -> p a b", a=6)
        b0p = carve(0, 8192, BF16)
        bup = carve(8192, 28672, U8, (7, 4096))
        qwt = carve(36864, 4096, I32, (8, 128))
        wlt = carve(40960, 8192, BF16)
        wht = carve(49152, 8192, BF16)
        l8t = carve(57344, 2048, F32, (2, 256))
        l4t = carve(59392, 128, F32, (2, 16))
        d8t = carve(59520, 1024, F32, (2, 128))
        d4t = carve(60544, 64, F32, (2, 8))
        tscr = {lv: carve(60608 + (lv - 1) * 4096, 4096, BF16) for lv in range(1, 8)}
        b0u8 = carve(85184, 4096, U8)
        wtf = carve(0, 32768, BF16, (2, 16 * 512)).rearrange("p a (b c) -> p a b c", b=16)
        yhb = carve(32768, 16384, F32, (4, T))
        ylb = carve(49152, 16384, F32, (4, T))
        dsc = carve(65536, 4096, F32)
        dmc = carve(69632, 4096, F32)
        x2f9 = carve(73728, 16384, F32, (4, T))

        pb = [E(nc.psum_tensor(f"pb{i}", [128, 512], F32)) for i in range(8)]

        s_xs = [E(nc.semaphore(f"s_xs{i}")) for i in range(2)]
        s_ws = [E(nc.semaphore(f"s_ws{i}")) for i in range(2)]
        s_ks = [E(nc.semaphore(f"s_ks{i}")) for i in range(2)]
        s_vs = [E(nc.semaphore(f"s_vs{i}")) for i in range(2)]
        s_f9 = [E(nc.semaphore(f"s_f9{i}")) for i in range(4)]
        s_qw = [E(nc.semaphore(f"s_qw{i}")) for i in range(2)]
        s_wtu = [E(nc.semaphore(f"s_wtu{i}")) for i in range(2)]
        s_wtw = [E(nc.semaphore(f"s_wtw{i}")) for i in range(2)]
        s_lut = E(nc.semaphore("s_lut"))
        s_lf = E(nc.semaphore("s_lf"))
        s_mg = E(nc.semaphore("s_mg"))
        s_pre = E(nc.semaphore("s_pre"))
        s_kstm = E(nc.semaphore("s_kstm"))
        s_vstm = [E(nc.semaphore(f"s_vstm{i}")) for i in range(2)]
        s_x2m = [E(nc.semaphore(f"s_x2m{i}")) for i in range(2)]
        s_wsm = [E(nc.semaphore(f"s_wsm{i}")) for i in range(2)]
        s_osm = [E(nc.semaphore(f"s_osm{i}")) for i in range(2)]
        s_dve = E(nc.semaphore("s_dve"))
        s_act = E(nc.semaphore("s_act"))
        s_pe = E(nc.semaphore("s_pe"))
        s_pool = E(nc.semaphore("s_pool"))
        s_cck = E(nc.semaphore("s_cck"))
        s_ccv = E(nc.semaphore("s_ccv"))
        s_ccl = E(nc.semaphore("s_ccl"))
        s_cch = E(nc.semaphore("s_cch"))
        s_ph = E(nc.semaphore("s_ph"))
        s_vf = E(nc.semaphore("s_vf"))
        s_bc = E(nc.semaphore("s_bc"))

        block = E(nc.Block())

        n = dict(dve=0, act=0, pe=0, pool=0, ph=0, vf=0)
        dcnt = {}

        def dinc(sem):
            key = sem.name
            dcnt[key] = dcnt.get(key, 0) + 16
            return dcnt[key]

        def dtot(sem):
            return dcnt.get(sem.name, 0)

        def barrier():
            n['ph'] += 5
            tgt = n['ph']

            @block.sync
            def _(sync):
                sync.nop().then_inc(s_ph, 1)
                sync.wait_ge(s_ph, tgt)

            @block.gpsimd
            def _(g):
                g.nop().then_inc(s_ph, 1)
                g.wait_ge(s_ph, tgt)

            @block.vector
            def _(v):
                v.nop().then_inc(s_ph, 1)
                v.wait_ge(s_ph, tgt)

            @block.scalar
            def _(sc):
                sc.nop().then_inc(s_ph, 1)
                sc.wait_ge(s_ph, tgt)

            @block.tensor
            def _(t):
                t.nop().then_inc(s_ph, 1)
                t.wait_ge(s_ph, tgt)

        @block.sync
        def _(sync):
            for dst_, src_ in [(idf, identf), (idb, identb), (onec, onecol_b),
                               (oner, onerow_f), (bqv, bqkv_c), (bob, bo_c), (fbb, fb_c)]:
                sync.dma_start(out=dst_[:], in_=src_[:, :]).then_inc(s_pre, 16)
                dinc(s_pre)
            sync.wait_ge(s_pre, dtot(s_pre))
        barrier()

        def stats_pass(src_dram, pe_wait=None, sync_waits=()):
            a0, p0 = n['act'], n['pe']
            xv = [None] * NFC

            @block.sync
            def _(sync):
                for sw in sync_waits:
                    sync.wait_ge(*sw)
                for fc in range(0, NFC, 2):
                    if fc >= 2:
                        sync.wait_ge(s_act, a0 + fc - 1)
                    sync.dma_start(out=xtile[:, 0, :],
                                   in_=src_dram[128 * fc:128 * (fc + 1), :]
                                   ).then_inc(s_xs[0], 16)
                    xv[fc] = dinc(s_xs[0])

            @block.gpsimd
            def _(g):
                for sw in sync_waits:
                    g.wait_ge(*sw)
                for fc in range(1, NFC, 2):
                    if fc >= 2:
                        g.wait_ge(s_act, a0 + fc - 1)
                    g.dma_start(out=xtile[:, 1, :],
                                in_=src_dram[128 * fc:128 * (fc + 1), :]
                                ).then_inc(s_xs[1], 16)
                    xv[fc] = dinc(s_xs[1])

            @block.scalar
            def _(sc):
                for fc in range(NFC):
                    sc.wait_ge(s_xs[fc % 2], xv[fc])
                    if fc >= 2:
                        sc.wait_ge(s_pe, p0 + fc - 1)
                    sc.activation(xcb[:, fc % 2, :], xtile[:, fc % 2, :], AF.Copy)
                    sc.activation(xsq[:, fc % 2, :], xtile[:, fc % 2, :],
                                  AF.Square).then_inc(s_act, 1)
            n['act'] += NFC

            @block.tensor
            def _(t):
                if pe_wait is not None:
                    t.wait_ge(*pe_wait)
                for fc in range(NFC):
                    t.wait_ge(s_act, a0 + fc + 1)
                    st_, sp_ = (fc == 0), (fc == NFC - 1)
                    t.matmul(pb[0][:1, :], onec[:], xcb[:, fc % 2, 0:512], start=st_, stop=sp_)
                    t.matmul(pb[1][:1, :], onec[:], xcb[:, fc % 2, 512:1024], start=st_, stop=sp_)
                    t.matmul(pb[2][:1, :], onec[:], xsq[:, fc % 2, 0:512], start=st_, stop=sp_)
                    t.matmul(pb[3][:1, :], onec[:], xsq[:, fc % 2, 512:1024],
                             start=st_, stop=sp_).then_inc(s_pe, 1)
            n['pe'] += NFC
            barrier()

        def finalize_pass(with_mask):
            a0, d0, p0 = n['act'], n['dve'], n['pe']

            @block.scalar
            def _(sc):
                sc.wait_ge(s_pe, p0)
                sc.activation(rows[:, 0, 0:512], pb[0][:1, :], AF.Copy)
                sc.activation(rows[:, 0, 512:1024], pb[1][:1, :], AF.Copy)
                sc.activation(rows[:, 1, 0:512], pb[2][:1, :], AF.Copy)
                sc.activation(rows[:, 1, 512:1024], pb[3][:1, :], AF.Copy).then_inc(s_act, 1)
            n['act'] += 1

            @block.vector
            def _(v):
                v.wait_ge(s_act, a0 + 1)
                v.tensor_scalar_mul(rows[:, 0, :], rows[:, 0, :], 1.0 / C)
                v.tensor_scalar_mul(rows[:, 1, :], rows[:, 1, :], 1.0 / C)
                v.tensor_mul(rows[:, 2, :], rows[:, 0, :], rows[:, 0, :])
                v.tensor_sub(rows[:, 1, :], rows[:, 1, :], rows[:, 2, :])
                if with_mask:
                    v.tensor_copy(rows[:, 5, :], rows[:, 1, :])
                v.tensor_scalar_add(rows[:, 1, :], rows[:, 1, :], EPS).then_inc(s_dve, 1)
            n['dve'] += 1

            @block.scalar
            def _(sc):
                sc.wait_ge(s_dve, d0 + 1)
                if with_mask:
                    sc.activation(rows[:, 3, :], rows[:, 1, :], AF.Sqrt)
                    sc.activation(rows[:, 2, :], rows[:, 5, :], AF.Sqrt).then_inc(s_act, 1)
                else:
                    sc.activation(rows[:, 3, :], rows[:, 1, :], AF.Sqrt).then_inc(s_act, 1)
            n['act'] += 1

            @block.vector
            def _(v):
                v.wait_ge(s_act, a0 + 2)
                li = v.reciprocal(rows[:, 4, :], rows[:, 3, :])
                n['vf'] += 1
                li.then_inc(s_vf, 1)
                v.wait_ge(s_vf, n['vf'])
                if with_mask:
                    v.tensor_mul(rows[:, 5, :], rows[:, 0, :], rows[:, 4, :])
                    v.tensor_mul(rows[:, 2, :], rows[:, 2, :], rows[:, 4, :])
                    v.tensor_scalar_mul(rows[:, 2, :], rows[:, 2, :], 64.0)
                    li = v.tensor_single_scalar(rows[:, 2, :], rows[:, 2, :], 64.0,
                                                ALU.is_gt)
                else:
                    li = v.tensor_mul(rows[:, 5, :], rows[:, 0, :], rows[:, 4, :])
                n['vf'] += 1
                li.then_inc(s_vf, 1)
                v.wait_ge(s_vf, n['vf'])
                v.tensor_copy(rows[:, 3, 0:8], rows[:, 3, 0:8]).then_inc(s_dve, 1)
            n['dve'] += 1

            @block.sync
            def _(sync):
                sync.wait_ge(s_dve, d0 + 2)
                sync.dma_start(out=rowd[0:1, :], in_=rows[:, 4, :]).then_inc(s_bc, 16)
                dinc(s_bc)
                sync.dma_start(out=rowd[1:2, :], in_=rows[:, 5, :]).then_inc(s_bc, 16)
                dinc(s_bc)
                if with_mask:
                    sync.dma_start(out=rowd[2:3, :], in_=rows[:, 2, :]).then_inc(s_bc, 16)
                    dinc(s_bc)
                sync.wait_ge(s_bc, dtot(s_bc))
                sync.dma_start(out=rbc[:, :],
                               in_=rowd[0:1, :].partition_broadcast(128)).then_inc(s_bc, 16)
                dinc(s_bc)
                sync.dma_start(out=mrbc[:, :],
                               in_=rowd[1:2, :].partition_broadcast(128)).then_inc(s_bc, 16)
                dinc(s_bc)
                if with_mask:
                    sync.dma_start(out=mkbc[:, :],
                                   in_=rowd[2:3, :].partition_broadcast(128)).then_inc(s_bc, 16)
                    dinc(s_bc)
                sync.wait_ge(s_bc, dtot(s_bc))
            barrier()

        def xln_build(src_dram):
            d0, bc0 = n['dve'], dtot(s_bc)
            xv = [None] * NFC

            @block.sync
            def _(sync):
                for fc in range(0, NFC, 2):
                    if fc >= 2:
                        sync.wait_ge(s_dve, d0 + fc - 1)
                    sync.dma_start(out=xtile[:, 0, :],
                                   in_=src_dram[128 * fc:128 * (fc + 1), :]
                                   ).then_inc(s_xs[0], 16)
                    xv[fc] = dinc(s_xs[0])

            @block.gpsimd
            def _(g):
                for fc in range(1, NFC, 2):
                    if fc >= 2:
                        g.wait_ge(s_dve, d0 + fc - 1)
                    g.dma_start(out=xtile[:, 1, :],
                                in_=src_dram[128 * fc:128 * (fc + 1), :]
                                ).then_inc(s_xs[1], 16)
                    xv[fc] = dinc(s_xs[1])

            @block.vector
            def _(v):
                v.wait_ge(s_bc, bc0)
                for fc in range(NFC):
                    v.wait_ge(s_xs[fc % 2], xv[fc])
                    v.tensor_mul(tmpf[:], xtile[:, fc % 2, :], rbc[:])
                    v.tensor_sub(xln[:, fc, :], tmpf[:], mrbc[:]).then_inc(s_dve, 1)
            n['dve'] += NFC
            barrier()

        s_dbg = E(nc.semaphore("s_dbg"))

        def dump(pairs):
            if not debug:
                return

            @block.sync
            def _(sync):
                for dst, src in pairs:
                    sync.dma_start(out=dbg[dst][:, :] if dst != "d_kfull" else dbg[dst][:, :],
                                   in_=src).then_inc(s_dbg, 16)
                    dinc(s_dbg)
                sync.wait_ge(s_dbg, dtot(s_dbg))
            barrier()

        with nc.named_scope("P1_stats1"):
            stats_pass(xT)
        with nc.named_scope("P2_fin1"):
            finalize_pass(with_mask=False)
        dump([("d_rbc", rbc[:, :]), ("d_mrbc", mrbc[:, :])])
        with nc.named_scope("P3a_xln1"):
            xln_build(xT)
        dump([("d_xln", xln[:, 0, :])])

        _p3scope = nc.enter_named_scope("P3b_qkv", False)
        og_seq = [2, 3, 4, 5, 0, 1]
        a0, p0, d0 = n['act'], n['pe'], n['dve']
        wv = [None] * (6 * NFC)

        @block.sync
        def _(sync):
            for j, og in enumerate(og_seq):
                for ic in range(NFC):
                    jj = j * NFC + ic
                    if jj % 2 != 0:
                        continue
                    if jj >= 2:
                        sync.wait_ge(s_act, a0 + jj - 1)
                    sync.dma_start(out=wstr[:, 0, :],
                                   in_=wqkv[128 * ic:128 * (ic + 1),
                                            512 * og:512 * (og + 1)]).then_inc(s_ws[0], 16)
                    wv[jj] = dinc(s_ws[0])

        @block.gpsimd
        def _(g):
            for j, og in enumerate(og_seq):
                for ic in range(NFC):
                    jj = j * NFC + ic
                    if jj % 2 != 1:
                        continue
                    if jj >= 2:
                        g.wait_ge(s_act, a0 + jj - 1)
                    g.dma_start(out=wstr[:, 1, :],
                                in_=wqkv[128 * ic:128 * (ic + 1),
                                         512 * og:512 * (og + 1)]).then_inc(s_ws[1], 16)
                    wv[jj] = dinc(s_ws[1])

        @block.scalar
        def _(sc):
            for jj in range(6 * NFC):
                sc.wait_ge(s_ws[jj % 2], wv[jj])
                if jj >= 2:
                    sc.wait_ge(s_pe, p0 + jj - 1)
                sc.activation(wstb[:, jj % 2, :], wstr[:, jj % 2, :],
                              AF.Copy).then_inc(s_act, 1)
        n['act'] += 6 * NFC

        @block.tensor
        def _(t):
            t.wait_ge(s_dve, d0)
            for j, og in enumerate(og_seq):
                for ic in range(NFC):
                    jj = j * NFC + ic
                    t.wait_ge(s_act, a0 + jj + 1)
                    if ic == 0 and j > 0:
                        t.wait_ge(s_dve, d0 + j)
                    st_, sp_ = (ic == 0), (ic == NFC - 1)
                    for of4 in range(4):
                        t.matmul(pb[2 * of4][:, :],
                                 wstb[:, jj % 2, 128 * of4:128 * (of4 + 1)],
                                 xln[:, ic, 0:512], start=st_, stop=sp_)
                        li = t.matmul(pb[2 * of4 + 1][:, :],
                                      wstb[:, jj % 2, 128 * of4:128 * (of4 + 1)],
                                      xln[:, ic, 512:1024], start=st_, stop=sp_)
                    li.then_inc(s_pe, 1)
        n['pe'] += 6 * NFC
        pvt0 = n['pe']

        @block.vector
        def _(v):
            for j, og in enumerate(og_seq):
                v.wait_ge(s_pe, p0 + (j + 1) * NFC)
                for of4 in range(4):
                    of = 4 * og + of4
                    if of < 8:
                        dst0, dst1 = q_sb[:, of, 0:512], q_sb[:, of, 512:1024]
                    elif of < 16:
                        dst0, dst1 = kstg[:, of - 8, 0:512], kstg[:, of - 8, 512:1024]
                    else:
                        vs_ = (j - 2) * 4 + of4
                        dst0, dst1 = vstage[:, vs_, 0:512], vstage[:, vs_, 512:1024]
                    v.tensor_scalar_add(dst0, pb[2 * of4][:, :], bqv[:, of:of + 1])
                    li = v.tensor_scalar_add(dst1, pb[2 * of4 + 1][:, :], bqv[:, of:of + 1])
                li.then_inc(s_dve, 1)
        n['dve'] += 6
        avt0 = n['act']

        @block.tensor
        def _(t):
            t.wait_ge(s_dve, d0 + 6)
            for vv in range(64):
                jv, r = divmod(vv, 32)
                of4, tt = divmod(r, 8)
                if vv >= 2:
                    t.wait_ge(s_act, avt0 + vv - 1)
                t.transpose(pb[6 + vv % 2][:, 0:64].bitcast(BF16),
                            vstage[:, jv * 4 + of4, 128 * tt:128 * (tt + 1)],
                            idb[:]).then_inc(s_pe, 1)
        n['pe'] += 64

        vst_rec = {}

        @block.sync
        def _(sync):
            sync.wait_ge(s_dve, d0 + 2)
            for kk in range(8):
                sync.dma_start(out=k_loc[128 * kk:128 * (kk + 1), :],
                               in_=kstg[:, kk, :]).then_inc(s_kstm, 16)
                dinc(s_kstm)
            for vv in range(64):
                jv, r = divmod(vv, 32)
                of4, tt = divmod(r, 8)
                gp = vv // 4
                sync.wait_ge(s_act, avt0 + vv + 1)
                sync.dma_start(out=v_loc[128 * tt:128 * (tt + 1),
                                         128 * (jv * 4 + of4):128 * (jv * 4 + of4 + 1)],
                               in_=vtile[:, vv % 8, :]).then_inc(s_vstm[gp % 2], 16)
                dinc(s_vstm[gp % 2])
                if vv % 4 == 3:
                    vst_rec[gp] = dtot(s_vstm[gp % 2])

        @block.scalar
        def _(sc):
            for vv in range(64):
                sc.wait_ge(s_pe, pvt0 + vv + 1)
                if vv >= 8:
                    gp = (vv - 8) // 4
                    sc.wait_ge(s_vstm[gp % 2], vst_rec[gp])
                sc.activation(vtile[:, vv % 8, :], pb[6 + vv % 2][:, 0:64].bitcast(BF16),
                              AF.Copy).then_inc(s_act, 1)
        n['act'] += 64
        barrier()
        dump([("d_q", q_sb[:, 0, :]), ("d_kloc", k_loc[0:128, :]),
              ("d_vloc", v_loc[0:128, :])])

        nc.leave_named_scope("P3b_qkv", _p3scope[0], False)
        _p4scope = nc.enter_named_scope("P4_kvag", False)
        @block.gpsimd
        def _(g):
            g.wait_ge(s_kstm, dtot(s_kstm))
            g.collective_compute(
                "AllGather", ALU.bypass,
                replica_groups=[[0, 1, 2, 3], [4, 5, 6, 7]],
                ins=[k_loc.ap().opt()], outs=[k_full.ap().opt()],
            ).then_inc(s_cck, 1)
            g.wait_ge(s_vstm[0], dtot(s_vstm[0]))
            g.wait_ge(s_vstm[1], dtot(s_vstm[1]))
            g.collective_compute(
                "AllGather", ALU.bypass,
                replica_groups=[[0, 1, 2, 3], [4, 5, 6, 7]],
                ins=[v_loc.ap().opt()], outs=[v_full.ap().opt()],
            ).then_inc(s_ccv, 1)

        nc.leave_named_scope("P4_kvag", _p4scope[0], False)
        _p5scope = nc.enter_named_scope("P5_attn", False)
        p5, a5, d5 = n['pe'], n['act'], n['dve']
        pe_pos_scores = {}
        pe_pos_tp = {}
        pe_pos_av = {}
        pe_pos_ao = {}
        _pc = p5
        for i in range(9):
            if i < 8:
                for tkp in range(4):
                    for dc in range(8):
                        _pc += 1
                        pe_pos_scores[(i, tkp, dc)] = _pc
            if i >= 1:
                for tkc in range(32):
                    _pc += 1
                    pe_pos_tp[(i - 1, tkc)] = _pc
                for tkc in range(32):
                    _pc += 1
                    pe_pos_av[(i - 1, tkc)] = _pc
                for dvc in range(8):
                    _pc += 1
                    pe_pos_ao[(i - 1, dvc)] = _pc
        n['pe'] = _pc
        act_pos_evac = {}
        act_pos_exp = {}
        act_pos_ptev = {}
        act_pos_aostg = {}
        act_pos_aoev = {}
        _ac = a5
        for i in range(9):
            if i < 8:
                for b in range(8):
                    _ac += 1
                    act_pos_evac[(i, b)] = _ac
                _ac += 1
                act_pos_exp[i] = _ac
            if i >= 1:
                for tkc in range(32):
                    _ac += 1
                    act_pos_ptev[(i - 1, tkc)] = _ac
                _ac += 1
                act_pos_aostg[i - 1] = _ac
                for dvc in range(8):
                    _ac += 1
                    act_pos_aoev[(i - 1, dvc)] = _ac
        AQ2TOT = _ac - a5

        def a_evac(qt, b):
            return act_pos_evac[(qt, b)]

        def a_exp(qt):
            return act_pos_exp[qt]

        def a_ptev(qt, tkc):
            return act_pos_ptev[(qt, tkc)]

        def a_aostg(qt):
            return act_pos_aostg[qt]

        def a_aoev(qt, dvc):
            return act_pos_aoev[(qt, dvc)]


        @block.sync
        def _(sync):
            sync.wait_ge(s_cck, 1)
            sync.wait_ge(s_dve, d5)
            for qt in range(8):
                for tkp in range(4):
                    j = qt * 4 + tkp
                    if j >= 2:
                        q2, t2 = divmod(j - 2, 4)
                        sync.wait_ge(s_pe, pe_pos_scores[(q2, t2, 7)])
                    sync.dma_start(
                        out=kst[:, j % 2, :, :],
                        in_=k_full[1024 * tkp:1024 * (tkp + 1), :].rearrange(
                            "(c p) n -> p c n", p=128)).then_inc(s_ks[j % 2], 16)
                    dinc(s_ks[j % 2])

        @block.gpsimd
        def _(g):
            g.wait_ge(s_ccv, 1)
            g.wait_ge(s_dve, d5)
            for qt in range(8):
                for vq in range(4):
                    j = qt * 4 + vq
                    if j >= 2:
                        q2, v2 = divmod(j - 2, 4)
                        g.wait_ge(s_pe, pe_pos_av[(q2, v2 * 8 + 7)])
                    g.dma_start(
                        out=vst[:, j % 2, :, :],
                        in_=v_full[1024 * vq:1024 * (vq + 1), :].rearrange(
                            "(c p) n -> p c n", p=128)).then_inc(s_vs[j % 2], 16)
                    dinc(s_vs[j % 2])

        @block.tensor
        def _(t):
            t.wait_ge(s_dve, d5)
            for i in range(9):
                if i < 8:
                    qt = i
                    for tkp in range(4):
                        jk = qt * 4 + tkp
                        if tkp == 2:
                            t.wait_ge(s_act, a_evac(qt, 3))
                        t.wait_ge(s_ks[jk % 2], 16 * (jk // 2 + 1))
                        for dc in range(8):
                            if qt > 0 and tkp < 2 and dc == 0:
                                t.wait_ge(s_act, a_evac(qt - 1, 7))
                            b = 2 * (tkp % 2)
                            st_, sp_ = (dc == 0), (dc == 7)
                            t.matmul(pb[b][:, :], q_sb[:, dc, 128 * qt:128 * (qt + 1)],
                                     kst[:, jk % 2, dc, 0:512], start=st_, stop=sp_)
                            t.matmul(pb[b + 1][:, :],
                                     q_sb[:, dc, 128 * qt:128 * (qt + 1)],
                                     kst[:, jk % 2, dc, 512:1024], start=st_,
                                     stop=sp_).then_inc(s_pe, 1)
                if i >= 1:
                    qt = i - 1
                    for tkc in range(32):
                        if tkc == 0:
                            t.wait_ge(s_act, a_exp(qt))
                            t.wait_ge(s_dve, d5 + qt * 2 + 2)
                            if qt >= 1:
                                t.wait_ge(s_act, a_aoev(qt - 1, 7))
                        if tkc >= 2:
                            t.wait_ge(s_act, a_ptev(qt, tkc - 2))
                        pr_ = probs if qt % 2 == 0 else probs2
                        t.transpose(pb[6 + tkc % 2][:, 0:64].bitcast(BF16),
                                    pr_[:, 128 * tkc:128 * (tkc + 1)],
                                    idb[:]).then_inc(s_pe, 1)
                    for tkc in range(32):
                        jv = qt * 4 + tkc // 8
                        if tkc % 8 == 0:
                            t.wait_ge(s_vs[jv % 2], 16 * (jv // 2 + 1))
                        if tkc == 0:
                            t.wait_ge(s_act, a_ptev(qt, 31))
                        st_, sp_ = (tkc == 0), (tkc == 31)
                        t.matmul(pb[4][:, :], probsT[:, tkc, :],
                                 vst[:, jv % 2, tkc % 8, 0:512], start=st_, stop=sp_)
                        t.matmul(pb[5][:, :], probsT[:, tkc, :],
                                 vst[:, jv % 2, tkc % 8, 512:1024],
                                 start=st_, stop=sp_).then_inc(s_pe, 1)
                    for dvc in range(8):
                        if dvc == 0:
                            t.wait_ge(s_act, a_aostg(qt))
                        if dvc >= 2:
                            t.wait_ge(s_act, a_aoev(qt, dvc - 2))
                        t.transpose(pb[6 + dvc % 2][:, 0:64].bitcast(BF16),
                                    aostg[:, 128 * dvc:128 * (dvc + 1)],
                                    idb[:]).then_inc(s_pe, 1)

        @block.vector
        def _(v):
            for qt in range(8):
                v.wait_ge(s_act, a_evac(qt, 7))
                li = v.reduce_max(rcol[:, 0:1], ssbf[:, :], axis=mybir.AxisListType.X)
                n['vf'] += 1
                li.then_inc(s_vf, 1)
                v.wait_ge(s_vf, n['vf'])
                v.tensor_scalar_mul(rcol[:, 1:2], rcol[:, 0:1], -1.0 / 32.0).then_inc(s_dve, 1)
                v.wait_ge(s_act, a_exp(qt))
                li = v.reciprocal(rcol[:, 0:1], dnp[:, 0:1])
                n['vf'] += 1
                li.then_inc(s_vf, 1)
                v.wait_ge(s_vf, n['vf'])
                pr_ = probs if qt % 2 == 0 else probs2
                v.tensor_scalar_mul(pr_[:], pr_[:], rcol[:, 0:1]).then_inc(s_dve, 1)
        n['dve'] += 8 * 2

        @block.scalar
        def _(sc):
            for i in range(9):
                if i < 8:
                    qt = i
                    for tkp in range(4):
                        for h in range(2):
                            b = 2 * (tkp % 2) + h
                            if qt >= 1 and tkp == 0 and h == 0:
                                sc.wait_ge(s_dve, d5 + (qt - 1) * 2 + 1)
                            sc.wait_ge(s_pe, pe_pos_scores[(qt, tkp, 7)])
                            sc.activation(ssb[:, tkp * 2 + h, :],
                                          pb[b][:, :], AF.Copy).then_inc(s_act, 1)
                    if qt >= 2:
                        sc.wait_ge(s_pe, pe_pos_tp[(qt - 2, 31)])
                    sc.wait_ge(s_dve, d5 + qt * 2 + 1)
                    pr_ = probs if qt % 2 == 0 else probs2
                    sc.activation(pr_[:, :], ssbf[:, :], AF.Exp, bias=rcol[:, 1:2],
                                  scale=1.0 / 32.0, accum_out=dnp[:, 0:1]).then_inc(s_act, 1)
                if i >= 1:
                    qt = i - 1
                    for tkc in range(32):
                        sc.wait_ge(s_pe, pe_pos_tp[(qt, tkc)])
                        sc.activation(probsT[:, tkc, :],
                                      pb[6 + tkc % 2][:, 0:64].bitcast(BF16),
                                      AF.Copy).then_inc(s_act, 1)
                    sc.wait_ge(s_pe, pe_pos_av[(qt, 31)])
                    sc.activation(aostg[:, 0:512], pb[4][:, :], AF.Copy)
                    sc.activation(aostg[:, 512:1024], pb[5][:, :], AF.Copy).then_inc(s_act, 1)
                    for dvc in range(8):
                        sc.wait_ge(s_pe, pe_pos_ao[(qt, dvc)])
                        sc.activation(aoT[:, dvc, 128 * qt:128 * (qt + 1)],
                                      pb[6 + dvc % 2][:, 0:64].bitcast(BF16),
                                      AF.Copy).then_inc(s_act, 1)
        n['act'] += AQ2TOT
        barrier()

        nc.leave_named_scope("P5_attn", _p5scope[0], False)
        _p6scope = nc.enter_named_scope("P6_wo", False)
        a6, p6, d6 = n['act'], n['pe'], n['dve']
        ws0b, ws1b = dtot(s_ws[0]), dtot(s_ws[1])
        wv6 = [(ws0b if j % 2 == 0 else ws1b) + 16 * (j // 2 + 1) for j in range(64)]
        x2b = [dtot(s_x2m[0]), dtot(s_x2m[1])]
        x2rec = {ocg: x2b[ocg % 2] + 64 * (ocg // 2 + 1) for ocg in range(8)}
        f9b = [dtot(s_f9[i]) for i in range(4)]
        xv6 = [f9b[oc % 4] + 16 * (oc // 4 + 1) for oc in range(32)]

        @block.sync
        def _(sync):
            for j in range(0, 64, 2):
                if j >= 2:
                    sync.wait_ge(s_act, a6 + j - 1)
                ocg, dvc = divmod(j, 8)
                sync.dma_start(out=wstr[:, 0, :],
                               in_=wo[128 * dvc:128 * (dvc + 1),
                                      512 * ocg:512 * (ocg + 1)]).then_inc(s_ws[0], 16)
                dinc(s_ws[0])

        @block.gpsimd
        def _(g):
            for j in range(1, 64, 2):
                if j >= 2:
                    g.wait_ge(s_act, a6 + j - 1)
                ocg, dvc = divmod(j, 8)
                g.dma_start(out=wstr[:, 1, :],
                            in_=wo[128 * dvc:128 * (dvc + 1),
                                   512 * ocg:512 * (ocg + 1)]).then_inc(s_ws[1], 16)
                dinc(s_ws[1])
                if j % 8 == 5:
                    ocg = j // 8
                    if ocg >= 1:
                        g.wait_ge(s_x2m[(ocg - 1) % 2], x2rec[ocg - 1])
                    for oc4 in range(4):
                        oc = ocg * 4 + oc4
                        g.dma_start(out=ylb[:, oc % 4, :],
                                    in_=xT[128 * oc:128 * (oc + 1), :]
                                    ).then_inc(s_f9[oc % 4], 16)
                        dinc(s_f9[oc % 4])

        @block.scalar
        def _(sc):
            for j in range(64):
                sc.wait_ge(s_ws[j % 2], wv6[j])
                if j >= 2:
                    sc.wait_ge(s_pe, p6 + j - 1)
                sc.activation(wstb[:, j % 2, :], wstr[:, j % 2, :],
                              AF.Copy).then_inc(s_act, 1)
                if j % 8 == 7:
                    ocg = j // 8
                    for oc4 in range(4):
                        oc = ocg * 4 + oc4
                        sc.wait_ge(s_dve, d6 + oc + 1)
                        sc.dma_start(out=x2d[128 * oc:128 * (oc + 1), :],
                                     in_=ylb[:, oc % 4, :]).then_inc(s_x2m[ocg % 2], 16)
                        dinc(s_x2m[ocg % 2])
        n['act'] += 64

        @block.tensor
        def _(t):
            t.wait_ge(s_act, a6)
            for ocg in range(8):
                for dvc in range(8):
                    j = ocg * 8 + dvc
                    t.wait_ge(s_act, a6 + j + 1)
                    if dvc == 0 and ocg > 0:
                        t.wait_ge(s_dve, d6 + ocg * 4)
                    st_, sp_ = (dvc == 0), (dvc == 7)
                    for oc4 in range(4):
                        t.matmul(pb[2 * oc4][:, :],
                                 wstb[:, j % 2, 128 * oc4:128 * (oc4 + 1)],
                                 aoT[:, dvc, 0:512], start=st_, stop=sp_)
                        li = t.matmul(pb[2 * oc4 + 1][:, :],
                                      wstb[:, j % 2, 128 * oc4:128 * (oc4 + 1)],
                                      aoT[:, dvc, 512:1024], start=st_, stop=sp_)
                    li.then_inc(s_pe, 1)
        n['pe'] += 64

        @block.vector
        def _(v):
            for ocg in range(8):
                v.wait_ge(s_pe, p6 + (ocg + 1) * 8)
                for oc4 in range(4):
                    oc = ocg * 4 + oc4
                    v.wait_ge(s_f9[oc % 4], xv6[oc])
                    v.scalar_tensor_tensor(tmpf[:, 0:512], pb[2 * oc4][:, :],
                                           1.0, ylb[:, oc % 4, 0:512], ALU.mult, ALU.add)
                    v.scalar_tensor_tensor(tmpf[:, 512:1024], pb[2 * oc4 + 1][:, :],
                                           1.0, ylb[:, oc % 4, 512:1024], ALU.mult, ALU.add)
                    v.tensor_scalar_add(ylb[:, oc % 4, :], tmpf[:],
                                        bob[:, oc:oc + 1]).then_inc(s_dve, 1)
        n['dve'] += NFC
        barrier()
        dump([("d_x2", x2d[0:128, :])])

        nc.leave_named_scope("P6_wo", _p6scope[0], False)
        _p7scope = nc.enter_named_scope("P78_ln2", False)
        stats_pass(x2d, pe_wait=(s_dve, n['dve']),
                   sync_waits=[(s_x2m[0], dtot(s_x2m[0])), (s_x2m[1], dtot(s_x2m[1]))])
        finalize_pass(with_mask=True)
        BCFIN = dtot(s_bc)
        dump([("d_mk", mkbc[:, :])])
        xln_build(x2d)
        dump([("d_xln2", xln[:, 0, :])])

        nc.leave_named_scope("P78_ln2", _p7scope[0], False)
        _pdscope = nc.enter_named_scope("Pd_deq", False)
        dve_d = n['dve']
        pd_pe0, pd_a0 = n['pe'], n['act']
        qwv, lutv = [], []
        rg8 = [[0, 1, 2, 3, 4, 5, 6, 7]]

        lf8p, lf4p, mg8p, mg4p, mg8_mid, mg8_end, mg4_mid, mg4_end = plan_trees()

        @block.sync
        def _(sync):
            for k in range(4):
                if k >= 1:
                    sync.wait_ge(s_dve, dve_d + 2 * k - 1)
                for p in range(8):
                    sync.dma_start(out=qwt[:, p, :],
                                   in_=qw_s[p, 128 * k:128 * (k + 1), :]
                                   ).then_inc(s_qw[k % 2], 16)
                    dinc(s_qw[k % 2])
                qwv.append(dtot(s_qw[k % 2]))
                if k >= 2:
                    sync.wait_ge(s_mg, mg4_end[k - 2])
                sync.dma_start(out=l4t[:, k % 2, :],
                               in_=l4_s[128 * k:128 * (k + 1), :]).then_inc(s_lut, 16)
                dinc(s_lut)
                sync.dma_start(out=l8t[:, k % 2, :],
                               in_=l8_s[128 * k:128 * (k + 1), :]).then_inc(s_lut, 16)
                dinc(s_lut)
                lutv.append(dtot(s_lut))

        @block.gpsimd
        def _(g):
            for g_ in range(64):
                k, rg = divmod(g_, 16)
                tb, it4 = divmod(rg, 8)
                dst3 = wh_loc if tb == 0 else wl_loc
                g.wait_ge(s_act, pd_a0 + 4 * g_ + 4)
                dview = dst3[k, :, :].rearrange("(a p q) n -> a p q n", a=2, p=128)
                g.dma_start(out=dview[it4 // 4, :, 4 * (it4 % 4):4 * (it4 % 4) + 4, :],
                            in_=wts[:, :, :]).then_inc(s_wsm[g_ % 2], 16)
                dinc(s_wsm[g_ % 2])
                if rg == 15:
                    g.wait_ge(s_wsm[0], 16 * (8 * k + 8))
                    g.wait_ge(s_wsm[1], 16 * (8 * k + 8))
                    g.collective_compute(
                        "AllGather", ALU.bypass, replica_groups=rg8,
                        ins=[wl_loc[k, :, :].opt()], outs=[wl_full[k, :, :].opt()],
                    ).then_inc(s_ccl, 1)
                    g.collective_compute(
                        "AllGather", ALU.bypass, replica_groups=rg8,
                        ins=[wh_loc[k, :, :].opt()], outs=[wh_full[k, :, :].opt()],
                    ).then_inc(s_cch, 1)

        talt = {1: xcb[:, :, :].rearrange("p a b -> p (a b)"),
                2: xsq[:, :, :].rearrange("p a b -> p (a b)"),
                3: evtf[:, :, :].rearrange("p a b -> p (a b)").bitcast(BF16)[:, 0:2048]}

        def w_ap(did, cc):
            sl = slice(2048 * cc, 2048 * (cc + 1))
            if did[0] == 'w8':
                return wht[:, sl]
            if did[0] == 'w4':
                return wlt[:, sl]
            lv, par = did[1], did[2]
            return talt[lv] if par == 1 else tscr[lv][:]

        @block.vector
        def _(v):
            for k in range(4):
                v.wait_ge(s_qw[k % 2], qwv[k])
                emit_bits_full(v, qwt, b0u8, b0p, bup).then_inc(s_dve, 1)
                v.wait_ge(s_lut, lutv[k])
                v.tensor_sub(d8t[:, k % 2, :], l8t[:, k % 2, 1::2], l8t[:, k % 2, 0::2])
                v.tensor_sub(d4t[:, k % 2, :], l4t[:, k % 2, 1::2],
                             l4t[:, k % 2, 0::2]).then_inc(s_dve, 1)
                for (wtag, mgl, pewait) in (('w8', mg8p[k], pd_pe0 + 64 * k - 32),
                                            ('w4', mg4p[k], pd_pe0 + 64 * k)):
                    waited = k < 1
                    for (did, lv, par, cc, lfneed) in mgl:
                        if not waited and did[0] == wtag:
                            v.wait_ge(s_pe, pewait)
                            waited = True
                        v.wait_ge(s_lf, lfneed)
                        sl = slice(2048 * cc, 2048 * (cc + 1))
                        v.copy_predicated(w_ap(did, cc), bup[:, lv - 1, sl],
                                          w_ap(('s', lv, par), cc)).then_inc(s_mg, 1)
        n['dve'] += 8

        @block.tensor
        def _(t):
            for m in range(256):
                k, r = divmod(m, 64)
                tb, it = divmod(r, 32)
                src = wht if tb == 0 else wlt
                if r == 0:
                    t.wait_ge(s_mg, mg8_mid[k])
                if r == 16:
                    t.wait_ge(s_mg, mg8_end[k])
                if r == 32:
                    t.wait_ge(s_mg, mg4_mid[k])
                if r == 48:
                    t.wait_ge(s_mg, mg4_end[k])
                if m >= 2:
                    t.wait_ge(s_act, pd_a0 + m - 1)
                t.transpose(pb[6 + m % 2][:, 0:64].bitcast(BF16),
                            src[:, 128 * it:128 * (it + 1)], idb[:]).then_inc(s_pe, 1)
        n['pe'] += 256

        @block.scalar
        def _(sc):
            ev = 0

            def emit_evac(m):
                sc.wait_ge(s_pe, pd_pe0 + m + 1)
                if m >= 4:
                    gp = m // 4 - 1
                    sc.wait_ge(s_wsm[gp % 2], 16 * (gp // 2 + 1))
                sc.activation(wts[:, m % 4, :], pb[6 + m % 2][:, 0:64].bitcast(BF16),
                              AF.Copy).then_inc(s_act, 1)

            for k in range(4):
                sc.wait_ge(s_dve, dve_d + 2 * k + 2)
                evhi = 64 * k
                cnt = 0
                for phase, leaves, pewait, evneed in (
                        ('w8', lf8p[k], pd_pe0 + 64 * k - 32, 64 * (k - 1) + 30),
                        ('w4', lf4p[k], pd_pe0 + 64 * k, 64 * (k - 1) + 62)):
                    waited = k < 1
                    for (did, cc, base, mgneed, tt) in leaves:
                        if not waited and did[0] == phase:
                            while ev < min(evneed, evhi):
                                emit_evac(ev)
                                ev += 1
                            sc.wait_ge(s_pe, pewait)
                            waited = True
                        if mgneed:
                            sc.wait_ge(s_mg, mgneed)
                        lt = l8t if tt == 8 else l4t
                        dt_ = d8t if tt == 8 else d4t
                        sc.activation(w_ap(did, cc), b0p[:, 2048 * cc:2048 * (cc + 1)],
                                      AF.Identity, bias=lt[:, k % 2, base:base + 1],
                                      scale=dt_[:, k % 2, base // 2:base // 2 + 1]
                                      ).then_inc(s_lf, 1)
                        cnt += 1
                        cap = evhi
                        if phase == 'w8' and cnt >= 140:
                            cap = 64 * k + 16
                        elif phase == 'w4' and cnt >= 262:
                            cap = 64 * k + 32
                        if cnt % 4 == 0 and ev < cap:
                            emit_evac(ev)
                            ev += 1
                while ev < evhi:
                    emit_evac(ev)
                    ev += 1
            while ev < 256:
                emit_evac(ev)
                ev += 1
        n['act'] += 256
        barrier()

        nc.leave_named_scope("Pd_deq", _pdscope[0], False)
        _pwscope = nc.enter_named_scope("Pw_wag", False)
        nc.leave_named_scope("Pw_wag", _pwscope[0], False)
        _p9scope = nc.enter_named_scope("P9_ffn", False)
        p9, a9, d9 = n['pe'], n['act'], n['dve']
        wtreq = {}
        wt_cnt = {('sc', sl): 0 for sl in range(2)} | {('g', sl): 0 for sl in range(2)}
        x9v = [None] * 32
        ost = {}

        def unit_mm_done(u):
            os_, r = divmod(u, 4)
            return p9 + os_ * 64 + (r + 1) * 16

        @block.sync
        def _(sync):
            for os_ in range(8):
                for oc4 in range(4):
                    j = os_ * 4 + oc4
                    if j >= 4:
                        sync.wait_ge(s_osm[(os_ - 1) % 2], ost[os_ - 1])
                    sync.dma_start(out=x2f9[:, oc4, :],
                                   in_=x2d[128 * j:128 * (j + 1), :]).then_inc(s_f9[j % 4], 16)
                    x9v[j] = dinc(s_f9[j % 4])
                for oc4 in range(4):
                    sync.wait_ge(s_dve, d9 + os_ * 4 + oc4 + 1)
                    oc = os_ * 4 + oc4
                    sync.dma_start(out=xout[128 * oc:128 * (oc + 1), :],
                                   in_=x2f9[:, oc4, :]).then_inc(s_osm[os_ % 2], 16)
                    dinc(s_osm[os_ % 2])
                ost[os_] = dtot(s_osm[os_ % 2])
            sync.wait_ge(s_osm[0], dtot(s_osm[0]))
            sync.wait_ge(s_osm[1], dtot(s_osm[1]))

        @block.scalar
        def _(sc):
            for os_ in range(8):
                for r in range(2):
                    u = os_ * 4 + r
                    ih = r
                    if u <= 1:
                        sc.wait_ge(s_ccl, 4)
                    if u >= 2:
                        sc.wait_ge(s_pe, unit_mm_done(u - 2))
                    for kq in range(4):
                        sc.dma_start(
                            out=wtf[:, u % 2, :, 128 * kq:128 * (kq + 1)],
                            in_=wl_full[kq, C * os_ + 2048 * ih:C * os_ + 2048 * (ih + 1),
                                        :].rearrange("(p q) n -> p q n", p=128)
                            ).then_inc(s_wtu[u % 2], 16)
                        dinc(s_wtu[u % 2])
                    wt_cnt[('sc', u % 2)] += 64
                    wtreq[u] = (s_wtu[u % 2], wt_cnt[('sc', u % 2)])
                sc.wait_ge(s_pe, p9 + os_ * 64 + 32)
                if os_ >= 1:
                    sc.wait_ge(s_dve, d9 + os_ * 4)
                for oc4 in range(4):
                    sc.activation(ylb[:, oc4, 0:512], pb[2 * oc4][:, :], AF.Copy)
                    sc.activation(ylb[:, oc4, 512:1024], pb[2 * oc4 + 1][:, :],
                                  AF.Copy).then_inc(s_act, 2)
                sc.wait_ge(s_pe, p9 + os_ * 64 + 64)
                for oc4 in range(4):
                    sc.activation(yhb[:, oc4, 0:512], pb[2 * oc4][:, :], AF.Copy)
                    sc.activation(yhb[:, oc4, 512:1024], pb[2 * oc4 + 1][:, :],
                                  AF.Copy).then_inc(s_act, 2)
        n['act'] += 8 * 16

        @block.gpsimd
        def _(g):
            for os_ in range(8):
                for r in range(2, 4):
                    u = os_ * 4 + r
                    ih = r - 2
                    if u <= 3:
                        g.wait_ge(s_cch, 4)
                    if u >= 2:
                        g.wait_ge(s_pe, unit_mm_done(u - 2))
                    for kq in range(4):
                        g.dma_start(
                            out=wtf[:, u % 2, :, 128 * kq:128 * (kq + 1)],
                            in_=wh_full[kq, C * os_ + 2048 * ih:C * os_ + 2048 * (ih + 1),
                                        :].rearrange("(p q) n -> p q n", p=128)
                            ).then_inc(s_wtw[u % 2], 16)
                        dinc(s_wtw[u % 2])
                    wt_cnt[('g', u % 2)] += 64
                    wtreq[u] = (s_wtw[u % 2], wt_cnt[('g', u % 2)])

        @block.tensor
        def _(t):
            t.wait_ge(s_dve, d9)
            for os_ in range(8):
                for table in range(2):
                    for ih in range(2):
                        u = os_ * 4 + table * 2 + ih
                        t.wait_ge(*wtreq[u])
                        if table == 0 and ih == 0 and os_ > 0:
                            t.wait_ge(s_act, a9 + os_ * 16)
                        if table == 1 and ih == 0:
                            t.wait_ge(s_act, a9 + os_ * 16 + 8)
                        for icq in range(16):
                            ic = ih * 16 + icq
                            st_, sp_ = (ic == 0), (ic == NFC - 1)
                            for oc4 in range(4):
                                t.matmul(pb[2 * oc4][:, :],
                                         wtf[:, u % 2, icq, 128 * oc4:128 * (oc4 + 1)],
                                         xln[:, ic, 0:512], start=st_, stop=sp_)
                                li = t.matmul(pb[2 * oc4 + 1][:, :],
                                              wtf[:, u % 2, icq, 128 * oc4:128 * (oc4 + 1)],
                                              xln[:, ic, 512:1024], start=st_, stop=sp_)
                            li.then_inc(s_pe, 1)
        n['pe'] += 8 * 64

        @block.vector
        def _(v):
            v.wait_ge(s_bc, BCFIN)
            for os_ in range(8):
                v.wait_ge(s_act, a9 + os_ * 16 + 16)
                for oc4 in range(4):
                    oc = os_ * 4 + oc4
                    j = os_ * 4 + oc4
                    v.wait_ge(s_f9[j % 4], x9v[j])
                    v.tensor_sub(dsc[:], yhb[:, oc4, :], ylb[:, oc4, :])
                    v.tensor_mul(dmc[:], dsc[:], mkbc[:])
                    v.tensor_scalar_add(tmpf[:], ylb[:, oc4, :], fbb[:, oc:oc + 1])
                    v.tensor_add(tmpf[:], tmpf[:], x2f9[:, oc4, :])
                    v.tensor_add(x2f9[:, oc4, :], tmpf[:], dmc[:]).then_inc(s_dve, 1)
        n['dve'] += 32
        nc.leave_named_scope("P9_ffn", _p9scope[0], False)

    return nc


def kernel(L0_attn_x, L0_attn_Wqkv, L0_attn_bqkv, L0_attn_Wo, L0_attn_bo,
           L0_ffn_qweight, L0_ffn_lut4, L0_ffn_lut8, L0_ffn_bias):
    import ml_dtypes
    nc = build_program()
    ident = np.eye(128, dtype=np.float32)
    identb = np.eye(128, dtype=ml_dtypes.bfloat16)
    onec = np.ones((128, 1), dtype=ml_dtypes.bfloat16)
    oner = np.ones((1, 128), dtype=np.float32)
    bqkv_c = np.ascontiguousarray(np.asarray(L0_attn_bqkv, np.float32).reshape(24, 128).T)
    bo_c = np.ascontiguousarray(np.asarray(L0_attn_bo, np.float32).reshape(32, 128).T)
    fb_c = np.ascontiguousarray(np.asarray(L0_ffn_bias, np.float32).reshape(32, 128).T)
    x = np.asarray(L0_attn_x, np.float32)
    qw = np.asarray(L0_ffn_qweight, np.int32)
    wq = np.ascontiguousarray(np.asarray(L0_attn_Wqkv, np.float32))
    wof = np.ascontiguousarray(np.asarray(L0_attn_Wo, np.float32))
    l4 = np.asarray(L0_ffn_lut4, np.float32)
    l8 = np.asarray(L0_ffn_lut8, np.float32)
    in_maps = []
    for c in range(8):
        g, q = divmod(c, 4)
        in_maps.append({
            "xT": np.ascontiguousarray(x[g, 1024 * q:1024 * (q + 1), :].T),
            "wqkv": wq,
            "bqkv_c": bqkv_c,
            "wo": wof,
            "bo_c": bo_c,
            "qw_s": np.ascontiguousarray(qw[:, 512 * c:512 * (c + 1), :]),
            "l4_s": np.ascontiguousarray(l4[512 * c:512 * (c + 1), :]),
            "l8_s": np.ascontiguousarray(l8[512 * c:512 * (c + 1), :]),
            "fb_c": fb_c,
            "identf": ident,
            "identb": identb,
            "onecol_b": onec,
            "onerow_f": oner,
        })
    res = run_bass_kernel_spmd(nc, in_maps, core_ids=list(range(8)))
    out = np.empty((2, 4096, 4096), np.float32)
    for c in range(8):
        g, q = divmod(c, 4)
        out[g, 1024 * q:1024 * (q + 1), :] = res.results[c]["xout"].T
    return out


# version 20
# speedup vs baseline: 1.1263x; 1.1263x over previous
import numpy as np
from contextlib import ExitStack

import concourse.bass as bass
import concourse.mybir as mybir
from concourse.bass_utils import run_bass_kernel_spmd

F32 = mybir.dt.float32
BF16 = mybir.dt.bfloat16
I32 = mybir.dt.int32
U8 = mybir.dt.uint8
AF = mybir.ActivationFunctionType
ALU = mybir.AluOpType

C = 4096
D = 1024
T = 1024
NFC = C // 128
OSL = 512
EPS = 1e-5
AQT = 49
PQT = 104


def emit_bits_full(v, qwt_ap, b0u8, b0p, bup):
    li = None
    for p in range(8):
        src = qwt_ap[:, p, :].bitcast(U8)
        for s in range(8):
            dst = b0u8[:, s::8] if p == 0 else bup[:, p - 1, s::8]
            li = v.tensor_scalar(dst, src, s, 1, ALU.logical_shift_right,
                                 ALU.bitwise_and)
    li = v.tensor_copy(b0p[:], b0u8[:])
    return li


def plan_trees():
    st = {'lf': 0, 'mg': 0, 'rd': {}, 'use': {lv: 0 for lv in range(1, 8)}}
    lf8 = {k: [] for k in range(4)}
    lf4 = {k: [] for k in range(4)}
    mg8 = {k: [] for k in range(4)}
    mg4 = {k: [] for k in range(4)}
    mg8_end, mg4_end = {}, {}

    def rec(nbits, base, did, cc, tt, lfl, mgl):
        if nbits == 1:
            st['lf'] += 1
            lfl.append((did, cc, base, st['rd'].get(did, 0), tt))
            return
        lv = nbits - 1
        par = st['use'][lv] % 2 if lv <= 3 else 0
        st['use'][lv] += 1
        sid = ('s', lv, par)
        half = 1 << lv
        rec(lv, base + half, sid, cc, tt, lfl, mgl)
        rec(lv, base, did, cc, tt, lfl, mgl)
        st['mg'] += 1
        mgl.append((did, lv, par, cc, st['lf']))
        st['rd'][sid] = st['mg']
        st['rd'][did] = st['mg']

    mg8_mid, mg4_mid = {}, {}
    for k in range(4):
        rec(8, 0, ('w8', 0), 0, 8, lf8[k], mg8[k])
        mg8_mid[k] = st['mg']
        rec(8, 0, ('w8', 1), 1, 8, lf8[k], mg8[k])
        mg8_end[k] = st['mg']
        rec(4, 0, ('w4', 0), 0, 4, lf4[k], mg4[k])
        mg4_mid[k] = st['mg']
        rec(4, 0, ('w4', 1), 1, 4, lf4[k], mg4[k])
        mg4_end[k] = st['mg']
    return lf8, lf4, mg8, mg4, mg8_mid, mg8_end, mg4_mid, mg4_end


def build_program(debug=False):
    nc = bass.Bass("TRN2", target_bir_lowering=False)

    xT = nc.dram_tensor("xT", [C, T], F32, kind="ExternalInput")
    wqkv = nc.dram_tensor("wqkv", [C, 3 * D], F32, kind="ExternalInput")
    bqkv_c = nc.dram_tensor("bqkv_c", [128, 24], F32, kind="ExternalInput")
    wo = nc.dram_tensor("wo", [D, C], F32, kind="ExternalInput")
    bo_c = nc.dram_tensor("bo_c", [128, 32], F32, kind="ExternalInput")
    qw_s = nc.dram_tensor("qw_s", [8, OSL, 128], I32, kind="ExternalInput")
    l4_s = nc.dram_tensor("l4_s", [OSL, 16], F32, kind="ExternalInput")
    l8_s = nc.dram_tensor("l8_s", [OSL, 256], F32, kind="ExternalInput")
    fb_c = nc.dram_tensor("fb_c", [128, 32], F32, kind="ExternalInput")
    identf = nc.dram_tensor("identf", [128, 128], F32, kind="ExternalInput")
    identb = nc.dram_tensor("identb", [128, 128], BF16, kind="ExternalInput")
    onecol_b = nc.dram_tensor("onecol_b", [128, 1], BF16, kind="ExternalInput")
    onerow_f = nc.dram_tensor("onerow_f", [1, 128], F32, kind="ExternalInput")

    xout = nc.dram_tensor("xout", [C, T], F32, kind="ExternalOutput")

    k_loc = nc.dram_tensor("k_loc", [D, T], BF16)
    v_loc = nc.dram_tensor("v_loc", [T, D], BF16)
    k_full = nc.dram_tensor("k_full", [4 * D, T], BF16)
    v_full = nc.dram_tensor("v_full", [4 * T, D], BF16)
    wl_loc = nc.dram_tensor("wl_loc", [4, C, 128], BF16)
    wh_loc = nc.dram_tensor("wh_loc", [4, C, 128], BF16)
    wl_full = nc.dram_tensor("wl_full", [4, 8 * C, 128], BF16)
    wh_full = nc.dram_tensor("wh_full", [4, 8 * C, 128], BF16)
    x2d = nc.dram_tensor("x2d", [C, T], F32)
    rowd = nc.dram_tensor("rowd", [3, T], F32)
    dbg = {}
    if debug:
        for nm, shp, dt_ in [("d_rbc", [128, T], F32), ("d_mrbc", [128, T], F32),
                             ("d_xln", [128, T], BF16), ("d_q", [128, T], BF16),
                             ("d_kloc", [128, T], BF16), ("d_vloc", [128, D], BF16),
                             ("d_kfull", [128, T], BF16), ("d_x2", [128, T], F32),
                             ("d_mk", [128, T], F32), ("d_wl", [128, 1024], BF16),
                             ("d_wh", [128, 1024], BF16), ("d_xln2", [128, T], BF16),
                             ("d_probs", [128, 4096], BF16), ("d_ao", [128, T], BF16)]:
            dbg[nm] = nc.dram_tensor(nm, shp, dt_, kind="ExternalOutput")

    with ExitStack() as stk:
        E = stk.enter_context
        xtile = E(nc.sbuf_tensor("xtile", [128, 2, T], F32))
        tmpf = E(nc.sbuf_tensor("tmpf", [128, T], F32))
        xcb = E(nc.sbuf_tensor("xcb", [128, 2, T], BF16))
        xsq = E(nc.sbuf_tensor("xsq", [128, 2, T], BF16))
        xln = E(nc.sbuf_tensor("xln", [128, NFC, T], BF16))
        kst = xln[:, 0:16, :].rearrange("p (a c) n -> p a c n", a=2)
        vst = xln[:, 16:32, :].rearrange("p (a c) n -> p a c n", a=2)
        pmax = E(nc.sbuf_tensor("pmax", [128, 8], F32))
        dnp = E(nc.sbuf_tensor("dnp", [128, 8], F32))
        rcol = E(nc.sbuf_tensor("rcol", [128, 2], F32))
        wstr = E(nc.sbuf_tensor("wstr", [128, 2, 512], F32))
        wstb = E(nc.sbuf_tensor("wstb", [128, 2, 512], BF16))
        rbc = E(nc.sbuf_tensor("rbc", [128, T], F32))
        mrbc = E(nc.sbuf_tensor("mrbc", [128, T], F32))
        mkbc = E(nc.sbuf_tensor("mkbc", [128, T], F32))
        bqv = E(nc.sbuf_tensor("bqv", [128, 24], F32))
        bob = E(nc.sbuf_tensor("bob", [128, 32], F32))
        fbb = E(nc.sbuf_tensor("fbb", [128, 32], F32))
        idf = E(nc.sbuf_tensor("idf", [128, 128], F32))
        idb = E(nc.sbuf_tensor("idb", [128, 128], BF16))
        onec = E(nc.sbuf_tensor("onec", [128, 1], BF16))
        oner = E(nc.sbuf_tensor("oner", [1, 128], F32))
        evtf = E(nc.sbuf_tensor("evtf", [128, 8, 128], F32))
        wts = E(nc.sbuf_tensor("wts", [128, 4, 128], BF16))
        shA = E(nc.sbuf_tensor("shA", [128, 90496], U8, align_bytes=128))

        def carve(off, nbytes, dt, shape=None):
            ap = shA[:, off:off + nbytes].bitcast(dt)
            if shape is not None:
                ap = ap.rearrange("p (a b) -> p a b", a=shape[0])
            return ap

        q_sb = carve(0, 16384, BF16, (8, T))
        kstg = carve(16384, 16384, BF16, (8, T))
        ssb = carve(16384, 16384, F32, (8, 512))
        ssbf = carve(16384, 16384, F32)
        vstage = carve(32768, 16384, BF16, (8, T))
        probs = carve(49152, 8192, BF16)
        probs2 = carve(32768, 8192, BF16)
        probsT = carve(57344, 8192, BF16, (32, 128))
        aoT = carve(65536, 16384, BF16, (8, T))
        aostg = carve(81920, 2048, BF16)
        vtile = carve(83968, 2048, BF16, (8, 128))
        rows = shA[0:1, 0:24576].bitcast(F32).rearrange("p (a b) -> p a b", a=6)
        b0p = carve(0, 8192, BF16)
        bup = carve(8192, 28672, U8, (7, 4096))
        qwt = carve(36864, 4096, I32, (8, 128))
        wlt = carve(40960, 8192, BF16)
        wht = carve(49152, 8192, BF16)
        l8t = carve(57344, 2048, F32, (2, 256))
        l4t = carve(59392, 128, F32, (2, 16))
        d8t = carve(59520, 1024, F32, (2, 128))
        d4t = carve(60544, 64, F32, (2, 8))
        tscr = {lv: carve(60608 + (lv - 1) * 4096, 4096, BF16) for lv in range(1, 8)}
        b0u8 = carve(85184, 4096, U8)
        wtf = carve(0, 32768, BF16, (2, 16 * 512)).rearrange("p a (b c) -> p a b c", b=16)
        yhb = carve(32768, 16384, F32, (4, T))
        ylb = carve(49152, 16384, F32, (4, T))
        dsc = carve(65536, 4096, F32)
        dmc = carve(69632, 4096, F32)
        x2f9 = carve(73728, 16384, F32, (4, T))

        pb = [E(nc.psum_tensor(f"pb{i}", [128, 512], F32)) for i in range(8)]

        s_xs = [E(nc.semaphore(f"s_xs{i}")) for i in range(2)]
        s_ws = [E(nc.semaphore(f"s_ws{i}")) for i in range(2)]
        s_ks = [E(nc.semaphore(f"s_ks{i}")) for i in range(2)]
        s_vs = [E(nc.semaphore(f"s_vs{i}")) for i in range(2)]
        s_f9 = [E(nc.semaphore(f"s_f9{i}")) for i in range(4)]
        s_qw = [E(nc.semaphore(f"s_qw{i}")) for i in range(2)]
        s_wtu = [E(nc.semaphore(f"s_wtu{i}")) for i in range(2)]
        s_wtw = [E(nc.semaphore(f"s_wtw{i}")) for i in range(2)]
        s_lut = E(nc.semaphore("s_lut"))
        s_lf = E(nc.semaphore("s_lf"))
        s_mg = E(nc.semaphore("s_mg"))
        s_pre = E(nc.semaphore("s_pre"))
        s_kstm = E(nc.semaphore("s_kstm"))
        s_vstm = [E(nc.semaphore(f"s_vstm{i}")) for i in range(2)]
        s_x2m = [E(nc.semaphore(f"s_x2m{i}")) for i in range(2)]
        s_wsm = [E(nc.semaphore(f"s_wsm{i}")) for i in range(2)]
        s_osm = [E(nc.semaphore(f"s_osm{i}")) for i in range(2)]
        s_dve = E(nc.semaphore("s_dve"))
        s_act = E(nc.semaphore("s_act"))
        s_pe = E(nc.semaphore("s_pe"))
        s_pool = E(nc.semaphore("s_pool"))
        s_cck = E(nc.semaphore("s_cck"))
        s_ccv = E(nc.semaphore("s_ccv"))
        s_ccl = E(nc.semaphore("s_ccl"))
        s_cch = E(nc.semaphore("s_cch"))
        s_ph = E(nc.semaphore("s_ph"))
        s_vf = E(nc.semaphore("s_vf"))
        s_bc = E(nc.semaphore("s_bc"))

        block = E(nc.Block())

        n = dict(dve=0, act=0, pe=0, pool=0, ph=0, vf=0)
        dcnt = {}

        def dinc(sem):
            key = sem.name
            dcnt[key] = dcnt.get(key, 0) + 16
            return dcnt[key]

        def dtot(sem):
            return dcnt.get(sem.name, 0)

        def barrier():
            n['ph'] += 5
            tgt = n['ph']

            @block.sync
            def _(sync):
                sync.nop().then_inc(s_ph, 1)
                sync.wait_ge(s_ph, tgt)

            @block.gpsimd
            def _(g):
                g.nop().then_inc(s_ph, 1)
                g.wait_ge(s_ph, tgt)

            @block.vector
            def _(v):
                v.nop().then_inc(s_ph, 1)
                v.wait_ge(s_ph, tgt)

            @block.scalar
            def _(sc):
                sc.nop().then_inc(s_ph, 1)
                sc.wait_ge(s_ph, tgt)

            @block.tensor
            def _(t):
                t.nop().then_inc(s_ph, 1)
                t.wait_ge(s_ph, tgt)

        @block.sync
        def _(sync):
            for dst_, src_ in [(idf, identf), (idb, identb), (onec, onecol_b),
                               (oner, onerow_f), (bqv, bqkv_c), (bob, bo_c), (fbb, fb_c)]:
                sync.dma_start(out=dst_[:], in_=src_[:, :]).then_inc(s_pre, 16)
                dinc(s_pre)
            sync.wait_ge(s_pre, dtot(s_pre))
        barrier()

        def stats_pass(src_dram, pe_wait=None, sync_waits=()):
            a0, p0 = n['act'], n['pe']
            xv = [None] * NFC

            @block.sync
            def _(sync):
                for sw in sync_waits:
                    sync.wait_ge(*sw)
                for fc in range(0, NFC, 2):
                    if fc >= 2:
                        sync.wait_ge(s_act, a0 + fc - 1)
                    sync.dma_start(out=xtile[:, 0, :],
                                   in_=src_dram[128 * fc:128 * (fc + 1), :]
                                   ).then_inc(s_xs[0], 16)
                    xv[fc] = dinc(s_xs[0])

            @block.gpsimd
            def _(g):
                for sw in sync_waits:
                    g.wait_ge(*sw)
                for fc in range(1, NFC, 2):
                    if fc >= 2:
                        g.wait_ge(s_act, a0 + fc - 1)
                    g.dma_start(out=xtile[:, 1, :],
                                in_=src_dram[128 * fc:128 * (fc + 1), :]
                                ).then_inc(s_xs[1], 16)
                    xv[fc] = dinc(s_xs[1])

            @block.scalar
            def _(sc):
                for fc in range(NFC):
                    sc.wait_ge(s_xs[fc % 2], xv[fc])
                    if fc >= 2:
                        sc.wait_ge(s_pe, p0 + fc - 1)
                    sc.activation(xcb[:, fc % 2, :], xtile[:, fc % 2, :], AF.Copy)
                    sc.activation(xsq[:, fc % 2, :], xtile[:, fc % 2, :],
                                  AF.Square).then_inc(s_act, 1)
            n['act'] += NFC

            @block.tensor
            def _(t):
                if pe_wait is not None:
                    t.wait_ge(*pe_wait)
                for fc in range(NFC):
                    t.wait_ge(s_act, a0 + fc + 1)
                    st_, sp_ = (fc == 0), (fc == NFC - 1)
                    t.matmul(pb[0][:1, :], onec[:], xcb[:, fc % 2, 0:512], start=st_, stop=sp_)
                    t.matmul(pb[1][:1, :], onec[:], xcb[:, fc % 2, 512:1024], start=st_, stop=sp_)
                    t.matmul(pb[2][:1, :], onec[:], xsq[:, fc % 2, 0:512], start=st_, stop=sp_)
                    t.matmul(pb[3][:1, :], onec[:], xsq[:, fc % 2, 512:1024],
                             start=st_, stop=sp_).then_inc(s_pe, 1)
            n['pe'] += NFC
            barrier()

        def finalize_pass(with_mask):
            a0, d0, p0 = n['act'], n['dve'], n['pe']

            @block.scalar
            def _(sc):
                sc.wait_ge(s_pe, p0)
                sc.activation(rows[:, 0, 0:512], pb[0][:1, :], AF.Copy)
                sc.activation(rows[:, 0, 512:1024], pb[1][:1, :], AF.Copy)
                sc.activation(rows[:, 1, 0:512], pb[2][:1, :], AF.Copy)
                sc.activation(rows[:, 1, 512:1024], pb[3][:1, :], AF.Copy).then_inc(s_act, 1)
            n['act'] += 1

            @block.vector
            def _(v):
                v.wait_ge(s_act, a0 + 1)
                v.tensor_scalar_mul(rows[:, 0, :], rows[:, 0, :], 1.0 / C)
                v.tensor_scalar_mul(rows[:, 1, :], rows[:, 1, :], 1.0 / C)
                v.tensor_mul(rows[:, 2, :], rows[:, 0, :], rows[:, 0, :])
                v.tensor_sub(rows[:, 1, :], rows[:, 1, :], rows[:, 2, :])
                if with_mask:
                    v.tensor_copy(rows[:, 5, :], rows[:, 1, :])
                v.tensor_scalar_add(rows[:, 1, :], rows[:, 1, :], EPS).then_inc(s_dve, 1)
            n['dve'] += 1

            @block.scalar
            def _(sc):
                sc.wait_ge(s_dve, d0 + 1)
                if with_mask:
                    sc.activation(rows[:, 3, :], rows[:, 1, :], AF.Sqrt)
                    sc.activation(rows[:, 2, :], rows[:, 5, :], AF.Sqrt).then_inc(s_act, 1)
                else:
                    sc.activation(rows[:, 3, :], rows[:, 1, :], AF.Sqrt).then_inc(s_act, 1)
            n['act'] += 1

            @block.vector
            def _(v):
                v.wait_ge(s_act, a0 + 2)
                li = v.reciprocal(rows[:, 4, :], rows[:, 3, :])
                n['vf'] += 1
                li.then_inc(s_vf, 1)
                v.wait_ge(s_vf, n['vf'])
                if with_mask:
                    v.tensor_mul(rows[:, 5, :], rows[:, 0, :], rows[:, 4, :])
                    v.tensor_mul(rows[:, 2, :], rows[:, 2, :], rows[:, 4, :])
                    v.tensor_scalar_mul(rows[:, 2, :], rows[:, 2, :], 64.0)
                    li = v.tensor_single_scalar(rows[:, 2, :], rows[:, 2, :], 64.0,
                                                ALU.is_gt)
                else:
                    li = v.tensor_mul(rows[:, 5, :], rows[:, 0, :], rows[:, 4, :])
                n['vf'] += 1
                li.then_inc(s_vf, 1)
                v.wait_ge(s_vf, n['vf'])
                v.tensor_copy(rows[:, 3, 0:8], rows[:, 3, 0:8]).then_inc(s_dve, 1)
            n['dve'] += 1

            @block.sync
            def _(sync):
                sync.wait_ge(s_dve, d0 + 2)
                sync.dma_start(out=rowd[0:1, :], in_=rows[:, 4, :]).then_inc(s_bc, 16)
                dinc(s_bc)
                sync.dma_start(out=rowd[1:2, :], in_=rows[:, 5, :]).then_inc(s_bc, 16)
                dinc(s_bc)
                if with_mask:
                    sync.dma_start(out=rowd[2:3, :], in_=rows[:, 2, :]).then_inc(s_bc, 16)
                    dinc(s_bc)
                sync.wait_ge(s_bc, dtot(s_bc))
                sync.dma_start(out=rbc[:, :],
                               in_=rowd[0:1, :].partition_broadcast(128)).then_inc(s_bc, 16)
                dinc(s_bc)
                sync.dma_start(out=mrbc[:, :],
                               in_=rowd[1:2, :].partition_broadcast(128)).then_inc(s_bc, 16)
                dinc(s_bc)
                if with_mask:
                    sync.dma_start(out=mkbc[:, :],
                                   in_=rowd[2:3, :].partition_broadcast(128)).then_inc(s_bc, 16)
                    dinc(s_bc)
                sync.wait_ge(s_bc, dtot(s_bc))
            barrier()

        def xln_build(src_dram):
            d0, bc0 = n['dve'], dtot(s_bc)
            xv = [None] * NFC

            @block.sync
            def _(sync):
                for fc in range(0, NFC, 2):
                    if fc >= 2:
                        sync.wait_ge(s_dve, d0 + fc - 1)
                    sync.dma_start(out=xtile[:, 0, :],
                                   in_=src_dram[128 * fc:128 * (fc + 1), :]
                                   ).then_inc(s_xs[0], 16)
                    xv[fc] = dinc(s_xs[0])

            @block.gpsimd
            def _(g):
                for fc in range(1, NFC, 2):
                    if fc >= 2:
                        g.wait_ge(s_dve, d0 + fc - 1)
                    g.dma_start(out=xtile[:, 1, :],
                                in_=src_dram[128 * fc:128 * (fc + 1), :]
                                ).then_inc(s_xs[1], 16)
                    xv[fc] = dinc(s_xs[1])

            @block.vector
            def _(v):
                v.wait_ge(s_bc, bc0)
                for fc in range(NFC):
                    v.wait_ge(s_xs[fc % 2], xv[fc])
                    v.tensor_mul(tmpf[:], xtile[:, fc % 2, :], rbc[:])
                    v.tensor_sub(xln[:, fc, :], tmpf[:], mrbc[:]).then_inc(s_dve, 1)
            n['dve'] += NFC
            barrier()

        s_dbg = E(nc.semaphore("s_dbg"))

        def dump(pairs):
            if not debug:
                return

            @block.sync
            def _(sync):
                for dst, src in pairs:
                    sync.dma_start(out=dbg[dst][:, :] if dst != "d_kfull" else dbg[dst][:, :],
                                   in_=src).then_inc(s_dbg, 16)
                    dinc(s_dbg)
                sync.wait_ge(s_dbg, dtot(s_dbg))
            barrier()

        with nc.named_scope("P1_stats1"):
            stats_pass(xT)
        with nc.named_scope("P2_fin1"):
            finalize_pass(with_mask=False)
        dump([("d_rbc", rbc[:, :]), ("d_mrbc", mrbc[:, :])])
        with nc.named_scope("P3a_xln1"):
            xln_build(xT)
        dump([("d_xln", xln[:, 0, :])])

        _p3scope = nc.enter_named_scope("P3b_qkv", False)
        og_seq = [2, 3, 4, 5, 0, 1]
        a0, p0, d0 = n['act'], n['pe'], n['dve']
        wv = [None] * (6 * NFC)

        @block.sync
        def _(sync):
            for j, og in enumerate(og_seq):
                for ic in range(NFC):
                    jj = j * NFC + ic
                    if jj % 2 != 0:
                        continue
                    if jj >= 2:
                        sync.wait_ge(s_act, a0 + jj - 1)
                    sync.dma_start(out=wstr[:, 0, :],
                                   in_=wqkv[128 * ic:128 * (ic + 1),
                                            512 * og:512 * (og + 1)]).then_inc(s_ws[0], 16)
                    wv[jj] = dinc(s_ws[0])

        @block.gpsimd
        def _(g):
            for j, og in enumerate(og_seq):
                for ic in range(NFC):
                    jj = j * NFC + ic
                    if jj % 2 != 1:
                        continue
                    if jj >= 2:
                        g.wait_ge(s_act, a0 + jj - 1)
                    g.dma_start(out=wstr[:, 1, :],
                                in_=wqkv[128 * ic:128 * (ic + 1),
                                         512 * og:512 * (og + 1)]).then_inc(s_ws[1], 16)
                    wv[jj] = dinc(s_ws[1])

        @block.scalar
        def _(sc):
            for jj in range(6 * NFC):
                sc.wait_ge(s_ws[jj % 2], wv[jj])
                if jj >= 2:
                    sc.wait_ge(s_pe, p0 + jj - 1)
                sc.activation(wstb[:, jj % 2, :], wstr[:, jj % 2, :],
                              AF.Copy).then_inc(s_act, 1)
        n['act'] += 6 * NFC

        @block.tensor
        def _(t):
            t.wait_ge(s_dve, d0)
            for j, og in enumerate(og_seq):
                for ic in range(NFC):
                    jj = j * NFC + ic
                    t.wait_ge(s_act, a0 + jj + 1)
                    if ic == 0 and j > 0:
                        t.wait_ge(s_dve, d0 + j)
                    st_, sp_ = (ic == 0), (ic == NFC - 1)
                    for of4 in range(4):
                        t.matmul(pb[2 * of4][:, :],
                                 wstb[:, jj % 2, 128 * of4:128 * (of4 + 1)],
                                 xln[:, ic, 0:512], start=st_, stop=sp_)
                        li = t.matmul(pb[2 * of4 + 1][:, :],
                                      wstb[:, jj % 2, 128 * of4:128 * (of4 + 1)],
                                      xln[:, ic, 512:1024], start=st_, stop=sp_)
                    li.then_inc(s_pe, 1)
        n['pe'] += 6 * NFC
        pvt0 = n['pe']

        @block.vector
        def _(v):
            for j, og in enumerate(og_seq):
                v.wait_ge(s_pe, p0 + (j + 1) * NFC)
                for of4 in range(4):
                    of = 4 * og + of4
                    if of < 8:
                        dst0, dst1 = q_sb[:, of, 0:512], q_sb[:, of, 512:1024]
                    elif of < 16:
                        dst0, dst1 = kstg[:, of - 8, 0:512], kstg[:, of - 8, 512:1024]
                    else:
                        vs_ = (j - 2) * 4 + of4
                        dst0, dst1 = vstage[:, vs_, 0:512], vstage[:, vs_, 512:1024]
                    v.tensor_scalar_add(dst0, pb[2 * of4][:, :], bqv[:, of:of + 1])
                    li = v.tensor_scalar_add(dst1, pb[2 * of4 + 1][:, :], bqv[:, of:of + 1])
                li.then_inc(s_dve, 1)
        n['dve'] += 6
        avt0 = n['act']

        @block.tensor
        def _(t):
            t.wait_ge(s_dve, d0 + 6)
            for vv in range(64):
                jv, r = divmod(vv, 32)
                of4, tt = divmod(r, 8)
                if vv >= 2:
                    t.wait_ge(s_act, avt0 + vv - 1)
                t.transpose(pb[6 + vv % 2][:, 0:64].bitcast(BF16),
                            vstage[:, jv * 4 + of4, 128 * tt:128 * (tt + 1)],
                            idb[:]).then_inc(s_pe, 1)
        n['pe'] += 64

        vst_rec = {}

        @block.sync
        def _(sync):
            sync.wait_ge(s_dve, d0 + 2)
            for kk in range(8):
                sync.dma_start(out=k_loc[128 * kk:128 * (kk + 1), :],
                               in_=kstg[:, kk, :]).then_inc(s_kstm, 16)
                dinc(s_kstm)
            for vv in range(64):
                jv, r = divmod(vv, 32)
                of4, tt = divmod(r, 8)
                gp = vv // 4
                sync.wait_ge(s_act, avt0 + vv + 1)
                sync.dma_start(out=v_loc[128 * tt:128 * (tt + 1),
                                         128 * (jv * 4 + of4):128 * (jv * 4 + of4 + 1)],
                               in_=vtile[:, vv % 8, :]).then_inc(s_vstm[gp % 2], 16)
                dinc(s_vstm[gp % 2])
                if vv % 4 == 3:
                    vst_rec[gp] = dtot(s_vstm[gp % 2])

        @block.scalar
        def _(sc):
            for vv in range(64):
                sc.wait_ge(s_pe, pvt0 + vv + 1)
                if vv >= 8:
                    gp = (vv - 8) // 4
                    sc.wait_ge(s_vstm[gp % 2], vst_rec[gp])
                sc.activation(vtile[:, vv % 8, :], pb[6 + vv % 2][:, 0:64].bitcast(BF16),
                              AF.Copy).then_inc(s_act, 1)
        n['act'] += 64
        barrier()
        dump([("d_q", q_sb[:, 0, :]), ("d_kloc", k_loc[0:128, :]),
              ("d_vloc", v_loc[0:128, :])])

        nc.leave_named_scope("P3b_qkv", _p3scope[0], False)
        _p4scope = nc.enter_named_scope("P4_kvag", False)
        @block.gpsimd
        def _(g):
            g.wait_ge(s_kstm, dtot(s_kstm))
            g.collective_compute(
                "AllGather", ALU.bypass,
                replica_groups=[[0, 1, 2, 3], [4, 5, 6, 7]],
                ins=[k_loc.ap().opt()], outs=[k_full.ap().opt()],
            ).then_inc(s_cck, 1)
            g.wait_ge(s_vstm[0], dtot(s_vstm[0]))
            g.wait_ge(s_vstm[1], dtot(s_vstm[1]))
            g.collective_compute(
                "AllGather", ALU.bypass,
                replica_groups=[[0, 1, 2, 3], [4, 5, 6, 7]],
                ins=[v_loc.ap().opt()], outs=[v_full.ap().opt()],
            ).then_inc(s_ccv, 1)

        nc.leave_named_scope("P4_kvag", _p4scope[0], False)
        _p5scope = nc.enter_named_scope("P5_attn", False)
        p5, a5, d5 = n['pe'], n['act'], n['dve']
        pe_pos_scores = {}
        pe_pos_tp = {}
        pe_pos_av = {}
        pe_pos_ao = {}
        _pc = p5
        for i in range(9):
            if i < 8:
                for tkp in range(4):
                    for dc in range(8):
                        _pc += 1
                        pe_pos_scores[(i, tkp, dc)] = _pc
            if i >= 1:
                for tkc in range(32):
                    _pc += 1
                    pe_pos_tp[(i - 1, tkc)] = _pc
                for tkc in range(32):
                    _pc += 1
                    pe_pos_av[(i - 1, tkc)] = _pc
                for dvc in range(8):
                    _pc += 1
                    pe_pos_ao[(i - 1, dvc)] = _pc
        n['pe'] = _pc
        act_pos_evac = {}
        act_pos_exp = {}
        act_pos_ptev = {}
        act_pos_aostg = {}
        act_pos_aoev = {}
        _ac = a5
        for i in range(9):
            if i < 8:
                for b in range(8):
                    _ac += 1
                    act_pos_evac[(i, b)] = _ac
                _ac += 1
                act_pos_exp[i] = _ac
            if i >= 1:
                for tkc in range(32):
                    _ac += 1
                    act_pos_ptev[(i - 1, tkc)] = _ac
                _ac += 1
                act_pos_aostg[i - 1] = _ac
                for dvc in range(8):
                    _ac += 1
                    act_pos_aoev[(i - 1, dvc)] = _ac
        AQ2TOT = _ac - a5

        def a_evac(qt, b):
            return act_pos_evac[(qt, b)]

        def a_exp(qt):
            return act_pos_exp[qt]

        def a_ptev(qt, tkc):
            return act_pos_ptev[(qt, tkc)]

        def a_aostg(qt):
            return act_pos_aostg[qt]

        def a_aoev(qt, dvc):
            return act_pos_aoev[(qt, dvc)]


        @block.sync
        def _(sync):
            sync.wait_ge(s_cck, 1)
            sync.wait_ge(s_dve, d5)
            for qt in range(8):
                for tkp in range(4):
                    j = qt * 4 + tkp
                    if j >= 2:
                        q2, t2 = divmod(j - 2, 4)
                        sync.wait_ge(s_pe, pe_pos_scores[(q2, t2, 7)])
                    if j < 2:
                        sync.dma_start(
                            out=kst[:, j % 2, 0:4, :],
                            in_=k_full[1024 * tkp:1024 * tkp + 512, :].rearrange(
                                "(c p) n -> p c n", p=128)).then_inc(s_ks[j % 2], 16)
                    else:
                        sync.dma_start(
                            out=kst[:, j % 2, :, :],
                            in_=k_full[1024 * tkp:1024 * (tkp + 1), :].rearrange(
                                "(c p) n -> p c n", p=128)).then_inc(s_ks[j % 2], 16)
                    dinc(s_ks[j % 2])

        @block.gpsimd
        def _(g):
            g.wait_ge(s_cck, 1)
            g.wait_ge(s_dve, d5)
            for j in range(2):
                g.dma_start(
                    out=kst[:, j % 2, 4:8, :],
                    in_=k_full[1024 * j + 512:1024 * (j + 1), :].rearrange(
                        "(c p) n -> p c n", p=128)).then_inc(s_ks[j % 2], 16)
                dinc(s_ks[j % 2])
            g.wait_ge(s_ccv, 1)
            for qt in range(8):
                for vq in range(4):
                    j = qt * 4 + vq
                    if j >= 2:
                        q2, v2 = divmod(j - 2, 4)
                        g.wait_ge(s_pe, pe_pos_av[(q2, v2 * 8 + 7)])
                    g.dma_start(
                        out=vst[:, j % 2, :, :],
                        in_=v_full[1024 * vq:1024 * (vq + 1), :].rearrange(
                            "(c p) n -> p c n", p=128)).then_inc(s_vs[j % 2], 16)
                    dinc(s_vs[j % 2])

        @block.tensor
        def _(t):
            t.wait_ge(s_dve, d5)
            for i in range(9):
                if i < 8:
                    qt = i
                    for tkp in range(4):
                        jk = qt * 4 + tkp
                        if tkp == 2:
                            t.wait_ge(s_act, a_evac(qt, 3))
                        t.wait_ge(s_ks[jk % 2], 16 * (jk // 2 + 1) + 16)
                        for dc in range(8):
                            if qt > 0 and tkp < 2 and dc == 0:
                                t.wait_ge(s_act, a_evac(qt - 1, 7))
                            b = 2 * (tkp % 2)
                            st_, sp_ = (dc == 0), (dc == 7)
                            t.matmul(pb[b][:, :], q_sb[:, dc, 128 * qt:128 * (qt + 1)],
                                     kst[:, jk % 2, dc, 0:512], start=st_, stop=sp_)
                            t.matmul(pb[b + 1][:, :],
                                     q_sb[:, dc, 128 * qt:128 * (qt + 1)],
                                     kst[:, jk % 2, dc, 512:1024], start=st_,
                                     stop=sp_).then_inc(s_pe, 1)
                if i >= 1:
                    qt = i - 1
                    for tkc in range(32):
                        if tkc == 0:
                            t.wait_ge(s_act, a_exp(qt))
                            t.wait_ge(s_dve, d5 + qt * 2 + 2)
                            if qt >= 1:
                                t.wait_ge(s_act, a_aoev(qt - 1, 7))
                        if tkc >= 2:
                            t.wait_ge(s_act, a_ptev(qt, tkc - 2))
                        pr_ = probs if qt % 2 == 0 else probs2
                        t.transpose(pb[6 + tkc % 2][:, 0:64].bitcast(BF16),
                                    pr_[:, 128 * tkc:128 * (tkc + 1)],
                                    idb[:]).then_inc(s_pe, 1)
                    for tkc in range(32):
                        jv = qt * 4 + tkc // 8
                        if tkc % 8 == 0:
                            t.wait_ge(s_vs[jv % 2], 16 * (jv // 2 + 1))
                        if tkc == 0:
                            t.wait_ge(s_act, a_ptev(qt, 31))
                        st_, sp_ = (tkc == 0), (tkc == 31)
                        t.matmul(pb[4][:, :], probsT[:, tkc, :],
                                 vst[:, jv % 2, tkc % 8, 0:512], start=st_, stop=sp_)
                        t.matmul(pb[5][:, :], probsT[:, tkc, :],
                                 vst[:, jv % 2, tkc % 8, 512:1024],
                                 start=st_, stop=sp_).then_inc(s_pe, 1)
                    for dvc in range(8):
                        if dvc == 0:
                            t.wait_ge(s_act, a_aostg(qt))
                        if dvc >= 2:
                            t.wait_ge(s_act, a_aoev(qt, dvc - 2))
                        t.transpose(pb[6 + dvc % 2][:, 0:64].bitcast(BF16),
                                    aostg[:, 128 * dvc:128 * (dvc + 1)],
                                    idb[:]).then_inc(s_pe, 1)

        @block.vector
        def _(v):
            for qt in range(8):
                v.wait_ge(s_act, a_evac(qt, 7))
                li = v.reduce_max(rcol[:, 0:1], ssbf[:, :], axis=mybir.AxisListType.X)
                n['vf'] += 1
                li.then_inc(s_vf, 1)
                v.wait_ge(s_vf, n['vf'])
                v.tensor_scalar_mul(rcol[:, 1:2], rcol[:, 0:1], -1.0 / 32.0).then_inc(s_dve, 1)
                v.wait_ge(s_act, a_exp(qt))
                li = v.reciprocal(rcol[:, 0:1], dnp[:, 0:1])
                n['vf'] += 1
                li.then_inc(s_vf, 1)
                v.wait_ge(s_vf, n['vf'])
                pr_ = probs if qt % 2 == 0 else probs2
                v.tensor_scalar_mul(pr_[:], pr_[:], rcol[:, 0:1]).then_inc(s_dve, 1)
        n['dve'] += 8 * 2

        @block.scalar
        def _(sc):
            for i in range(9):
                if i < 8:
                    qt = i
                    for tkp in range(4):
                        for h in range(2):
                            b = 2 * (tkp % 2) + h
                            if qt >= 1 and tkp == 0 and h == 0:
                                sc.wait_ge(s_dve, d5 + (qt - 1) * 2 + 1)
                            sc.wait_ge(s_pe, pe_pos_scores[(qt, tkp, 7)])
                            sc.activation(ssb[:, tkp * 2 + h, :],
                                          pb[b][:, :], AF.Copy).then_inc(s_act, 1)
                    if qt >= 2:
                        sc.wait_ge(s_pe, pe_pos_tp[(qt - 2, 31)])
                    sc.wait_ge(s_dve, d5 + qt * 2 + 1)
                    pr_ = probs if qt % 2 == 0 else probs2
                    sc.activation(pr_[:, :], ssbf[:, :], AF.Exp, bias=rcol[:, 1:2],
                                  scale=1.0 / 32.0, accum_out=dnp[:, 0:1]).then_inc(s_act, 1)
                if i >= 1:
                    qt = i - 1
                    for tkc in range(32):
                        sc.wait_ge(s_pe, pe_pos_tp[(qt, tkc)])
                        sc.activation(probsT[:, tkc, :],
                                      pb[6 + tkc % 2][:, 0:64].bitcast(BF16),
                                      AF.Copy).then_inc(s_act, 1)
                    sc.wait_ge(s_pe, pe_pos_av[(qt, 31)])
                    sc.activation(aostg[:, 0:512], pb[4][:, :], AF.Copy)
                    sc.activation(aostg[:, 512:1024], pb[5][:, :], AF.Copy).then_inc(s_act, 1)
                    for dvc in range(8):
                        sc.wait_ge(s_pe, pe_pos_ao[(qt, dvc)])
                        sc.activation(aoT[:, dvc, 128 * qt:128 * (qt + 1)],
                                      pb[6 + dvc % 2][:, 0:64].bitcast(BF16),
                                      AF.Copy).then_inc(s_act, 1)
        n['act'] += AQ2TOT
        barrier()

        nc.leave_named_scope("P5_attn", _p5scope[0], False)
        _p6scope = nc.enter_named_scope("P6_wo", False)
        a6, p6, d6 = n['act'], n['pe'], n['dve']
        ws0b, ws1b = dtot(s_ws[0]), dtot(s_ws[1])
        wv6 = [(ws0b if j % 2 == 0 else ws1b) + 16 * (j // 2 + 1) for j in range(64)]
        x2b = [dtot(s_x2m[0]), dtot(s_x2m[1])]
        x2rec = {ocg: x2b[ocg % 2] + 64 * (ocg // 2 + 1) for ocg in range(8)}
        f9b = [dtot(s_f9[i]) for i in range(4)]
        xv6 = [f9b[oc % 4] + 16 * (oc // 4 + 1) for oc in range(32)]

        @block.sync
        def _(sync):
            for j in range(0, 64, 2):
                if j >= 2:
                    sync.wait_ge(s_act, a6 + j - 1)
                ocg, dvc = divmod(j, 8)
                sync.dma_start(out=wstr[:, 0, :],
                               in_=wo[128 * dvc:128 * (dvc + 1),
                                      512 * ocg:512 * (ocg + 1)]).then_inc(s_ws[0], 16)
                dinc(s_ws[0])

        @block.gpsimd
        def _(g):
            for j in range(1, 64, 2):
                if j >= 2:
                    g.wait_ge(s_act, a6 + j - 1)
                ocg, dvc = divmod(j, 8)
                g.dma_start(out=wstr[:, 1, :],
                            in_=wo[128 * dvc:128 * (dvc + 1),
                                   512 * ocg:512 * (ocg + 1)]).then_inc(s_ws[1], 16)
                dinc(s_ws[1])
                if j % 8 == 5:
                    ocg = j // 8
                    if ocg >= 1:
                        g.wait_ge(s_x2m[(ocg - 1) % 2], x2rec[ocg - 1])
                    for oc4 in range(4):
                        oc = ocg * 4 + oc4
                        g.dma_start(out=ylb[:, oc % 4, :],
                                    in_=xT[128 * oc:128 * (oc + 1), :]
                                    ).then_inc(s_f9[oc % 4], 16)
                        dinc(s_f9[oc % 4])

        @block.scalar
        def _(sc):
            for j in range(64):
                sc.wait_ge(s_ws[j % 2], wv6[j])
                if j >= 2:
                    sc.wait_ge(s_pe, p6 + j - 1)
                sc.activation(wstb[:, j % 2, :], wstr[:, j % 2, :],
                              AF.Copy).then_inc(s_act, 1)
                if j % 8 == 7:
                    ocg = j // 8
                    for oc4 in range(4):
                        oc = ocg * 4 + oc4
                        sc.wait_ge(s_dve, d6 + oc + 1)
                        sc.dma_start(out=x2d[128 * oc:128 * (oc + 1), :],
                                     in_=ylb[:, oc % 4, :]).then_inc(s_x2m[ocg % 2], 16)
                        dinc(s_x2m[ocg % 2])
        n['act'] += 64

        @block.tensor
        def _(t):
            t.wait_ge(s_act, a6)
            for ocg in range(8):
                for dvc in range(8):
                    j = ocg * 8 + dvc
                    t.wait_ge(s_act, a6 + j + 1)
                    if dvc == 0 and ocg > 0:
                        t.wait_ge(s_dve, d6 + ocg * 4)
                    st_, sp_ = (dvc == 0), (dvc == 7)
                    for oc4 in range(4):
                        t.matmul(pb[2 * oc4][:, :],
                                 wstb[:, j % 2, 128 * oc4:128 * (oc4 + 1)],
                                 aoT[:, dvc, 0:512], start=st_, stop=sp_)
                        li = t.matmul(pb[2 * oc4 + 1][:, :],
                                      wstb[:, j % 2, 128 * oc4:128 * (oc4 + 1)],
                                      aoT[:, dvc, 512:1024], start=st_, stop=sp_)
                    li.then_inc(s_pe, 1)
        n['pe'] += 64

        @block.vector
        def _(v):
            for ocg in range(8):
                v.wait_ge(s_pe, p6 + (ocg + 1) * 8)
                for oc4 in range(4):
                    oc = ocg * 4 + oc4
                    v.wait_ge(s_f9[oc % 4], xv6[oc])
                    v.scalar_tensor_tensor(tmpf[:, 0:512], pb[2 * oc4][:, :],
                                           1.0, ylb[:, oc % 4, 0:512], ALU.mult, ALU.add)
                    v.scalar_tensor_tensor(tmpf[:, 512:1024], pb[2 * oc4 + 1][:, :],
                                           1.0, ylb[:, oc % 4, 512:1024], ALU.mult, ALU.add)
                    v.tensor_scalar_add(ylb[:, oc % 4, :], tmpf[:],
                                        bob[:, oc:oc + 1]).then_inc(s_dve, 1)
        n['dve'] += NFC
        barrier()
        dump([("d_x2", x2d[0:128, :])])

        nc.leave_named_scope("P6_wo", _p6scope[0], False)
        _p7scope = nc.enter_named_scope("P78_ln2", False)
        stats_pass(x2d, pe_wait=(s_dve, n['dve']),
                   sync_waits=[(s_x2m[0], dtot(s_x2m[0])), (s_x2m[1], dtot(s_x2m[1]))])
        finalize_pass(with_mask=True)
        BCFIN = dtot(s_bc)
        dump([("d_mk", mkbc[:, :])])
        xln_build(x2d)
        dump([("d_xln2", xln[:, 0, :])])

        nc.leave_named_scope("P78_ln2", _p7scope[0], False)
        _pdscope = nc.enter_named_scope("Pd_deq", False)
        dve_d = n['dve']
        pd_pe0, pd_a0 = n['pe'], n['act']
        qwv, lutv = [], []
        rg8 = [[0, 1, 2, 3, 4, 5, 6, 7]]

        lf8p, lf4p, mg8p, mg4p, mg8_mid, mg8_end, mg4_mid, mg4_end = plan_trees()

        @block.sync
        def _(sync):
            for k in range(4):
                if k >= 1:
                    sync.wait_ge(s_dve, dve_d + 2 * k - 1)
                for p in range(8):
                    sync.dma_start(out=qwt[:, p, :],
                                   in_=qw_s[p, 128 * k:128 * (k + 1), :]
                                   ).then_inc(s_qw[k % 2], 16)
                    dinc(s_qw[k % 2])
                qwv.append(dtot(s_qw[k % 2]))
                if k >= 2:
                    sync.wait_ge(s_mg, mg4_end[k - 2])
                sync.dma_start(out=l4t[:, k % 2, :],
                               in_=l4_s[128 * k:128 * (k + 1), :]).then_inc(s_lut, 16)
                dinc(s_lut)
                sync.dma_start(out=l8t[:, k % 2, :],
                               in_=l8_s[128 * k:128 * (k + 1), :]).then_inc(s_lut, 16)
                dinc(s_lut)
                lutv.append(dtot(s_lut))

        @block.gpsimd
        def _(g):
            for g_ in range(64):
                k, rg = divmod(g_, 16)
                tb, it4 = divmod(rg, 8)
                dst3 = wh_loc if tb == 0 else wl_loc
                g.wait_ge(s_act, pd_a0 + 4 * g_ + 4)
                dview = dst3[k, :, :].rearrange("(a p q) n -> a p q n", a=2, p=128)
                g.dma_start(out=dview[it4 // 4, :, 4 * (it4 % 4):4 * (it4 % 4) + 4, :],
                            in_=wts[:, :, :]).then_inc(s_wsm[g_ % 2], 16)
                dinc(s_wsm[g_ % 2])
                if rg == 15:
                    g.wait_ge(s_wsm[0], 16 * (8 * k + 8))
                    g.wait_ge(s_wsm[1], 16 * (8 * k + 8))
                    g.collective_compute(
                        "AllGather", ALU.bypass, replica_groups=rg8,
                        ins=[wl_loc[k, :, :].opt()], outs=[wl_full[k, :, :].opt()],
                    ).then_inc(s_ccl, 1)
                    g.collective_compute(
                        "AllGather", ALU.bypass, replica_groups=rg8,
                        ins=[wh_loc[k, :, :].opt()], outs=[wh_full[k, :, :].opt()],
                    ).then_inc(s_cch, 1)

        talt = {1: xcb[:, :, :].rearrange("p a b -> p (a b)"),
                2: xsq[:, :, :].rearrange("p a b -> p (a b)"),
                3: evtf[:, :, :].rearrange("p a b -> p (a b)").bitcast(BF16)[:, 0:2048]}

        def w_ap(did, cc):
            sl = slice(2048 * cc, 2048 * (cc + 1))
            if did[0] == 'w8':
                return wht[:, sl]
            if did[0] == 'w4':
                return wlt[:, sl]
            lv, par = did[1], did[2]
            return talt[lv] if par == 1 else tscr[lv][:]

        @block.vector
        def _(v):
            for k in range(4):
                v.wait_ge(s_qw[k % 2], qwv[k])
                emit_bits_full(v, qwt, b0u8, b0p, bup).then_inc(s_dve, 1)
                v.wait_ge(s_lut, lutv[k])
                v.tensor_sub(d8t[:, k % 2, :], l8t[:, k % 2, 1::2], l8t[:, k % 2, 0::2])
                v.tensor_sub(d4t[:, k % 2, :], l4t[:, k % 2, 1::2],
                             l4t[:, k % 2, 0::2]).then_inc(s_dve, 1)
                for (wtag, mgl, pewait) in (('w8', mg8p[k], pd_pe0 + 64 * k - 32),
                                            ('w4', mg4p[k], pd_pe0 + 64 * k)):
                    waited = k < 1
                    for (did, lv, par, cc, lfneed) in mgl:
                        if not waited and did[0] == wtag:
                            v.wait_ge(s_pe, pewait)
                            waited = True
                        v.wait_ge(s_lf, lfneed)
                        sl = slice(2048 * cc, 2048 * (cc + 1))
                        v.copy_predicated(w_ap(did, cc), bup[:, lv - 1, sl],
                                          w_ap(('s', lv, par), cc)).then_inc(s_mg, 1)
        n['dve'] += 8

        @block.tensor
        def _(t):
            for m in range(256):
                k, r = divmod(m, 64)
                tb, it = divmod(r, 32)
                src = wht if tb == 0 else wlt
                if r == 0:
                    t.wait_ge(s_mg, mg8_mid[k])
                if r == 16:
                    t.wait_ge(s_mg, mg8_end[k])
                if r == 32:
                    t.wait_ge(s_mg, mg4_mid[k])
                if r == 48:
                    t.wait_ge(s_mg, mg4_end[k])
                if m >= 2:
                    t.wait_ge(s_act, pd_a0 + m - 1)
                t.transpose(pb[6 + m % 2][:, 0:64].bitcast(BF16),
                            src[:, 128 * it:128 * (it + 1)], idb[:]).then_inc(s_pe, 1)
        n['pe'] += 256

        @block.scalar
        def _(sc):
            ev = 0

            def emit_evac(m):
                sc.wait_ge(s_pe, pd_pe0 + m + 1)
                if m >= 4:
                    gp = m // 4 - 1
                    sc.wait_ge(s_wsm[gp % 2], 16 * (gp // 2 + 1))
                sc.activation(wts[:, m % 4, :], pb[6 + m % 2][:, 0:64].bitcast(BF16),
                              AF.Copy).then_inc(s_act, 1)

            for k in range(4):
                sc.wait_ge(s_dve, dve_d + 2 * k + 2)
                evhi = 64 * k
                cnt = 0
                for phase, leaves, pewait, evneed in (
                        ('w8', lf8p[k], pd_pe0 + 64 * k - 32, 64 * (k - 1) + 30),
                        ('w4', lf4p[k], pd_pe0 + 64 * k, 64 * (k - 1) + 62)):
                    waited = k < 1
                    for (did, cc, base, mgneed, tt) in leaves:
                        if not waited and did[0] == phase:
                            while ev < min(evneed, evhi):
                                emit_evac(ev)
                                ev += 1
                            sc.wait_ge(s_pe, pewait)
                            waited = True
                        if mgneed:
                            sc.wait_ge(s_mg, mgneed)
                        lt = l8t if tt == 8 else l4t
                        dt_ = d8t if tt == 8 else d4t
                        sc.activation(w_ap(did, cc), b0p[:, 2048 * cc:2048 * (cc + 1)],
                                      AF.Identity, bias=lt[:, k % 2, base:base + 1],
                                      scale=dt_[:, k % 2, base // 2:base // 2 + 1]
                                      ).then_inc(s_lf, 1)
                        cnt += 1
                        cap = evhi
                        if phase == 'w8' and cnt >= 140:
                            cap = 64 * k + 16
                        elif phase == 'w4' and cnt >= 262:
                            cap = 64 * k + 32
                        if cnt % 4 == 0 and ev < cap:
                            emit_evac(ev)
                            ev += 1
                while ev < evhi:
                    emit_evac(ev)
                    ev += 1
            while ev < 256:
                emit_evac(ev)
                ev += 1
        n['act'] += 256
        barrier()

        nc.leave_named_scope("Pd_deq", _pdscope[0], False)
        _pwscope = nc.enter_named_scope("Pw_wag", False)
        nc.leave_named_scope("Pw_wag", _pwscope[0], False)
        _p9scope = nc.enter_named_scope("P9_ffn", False)
        p9, a9, d9 = n['pe'], n['act'], n['dve']
        wtreq = {}
        wt_cnt = {('sc', sl): 0 for sl in range(2)} | {('g', sl): 0 for sl in range(2)}
        x9v = [None] * 32
        ost = {}

        def unit_mm_done(u):
            os_, r = divmod(u, 4)
            return p9 + os_ * 64 + (r + 1) * 16

        @block.sync
        def _(sync):
            for os_ in range(8):
                for oc4 in range(4):
                    j = os_ * 4 + oc4
                    if j >= 4:
                        sync.wait_ge(s_osm[(os_ - 1) % 2], ost[os_ - 1])
                    sync.dma_start(out=x2f9[:, oc4, :],
                                   in_=x2d[128 * j:128 * (j + 1), :]).then_inc(s_f9[j % 4], 16)
                    x9v[j] = dinc(s_f9[j % 4])
                for oc4 in range(4):
                    sync.wait_ge(s_dve, d9 + os_ * 4 + oc4 + 1)
                    oc = os_ * 4 + oc4
                    sync.dma_start(out=xout[128 * oc:128 * (oc + 1), :],
                                   in_=x2f9[:, oc4, :]).then_inc(s_osm[os_ % 2], 16)
                    dinc(s_osm[os_ % 2])
                ost[os_] = dtot(s_osm[os_ % 2])
            sync.wait_ge(s_osm[0], dtot(s_osm[0]))
            sync.wait_ge(s_osm[1], dtot(s_osm[1]))

        @block.scalar
        def _(sc):
            for os_ in range(8):
                for r in range(2):
                    u = os_ * 4 + r
                    ih = r
                    if u <= 1:
                        sc.wait_ge(s_ccl, 4)
                    if u >= 2:
                        sc.wait_ge(s_pe, unit_mm_done(u - 2))
                    for kq in range(4):
                        sc.dma_start(
                            out=wtf[:, u % 2, :, 128 * kq:128 * (kq + 1)],
                            in_=wl_full[kq, C * os_ + 2048 * ih:C * os_ + 2048 * (ih + 1),
                                        :].rearrange("(p q) n -> p q n", p=128)
                            ).then_inc(s_wtu[u % 2], 16)
                        dinc(s_wtu[u % 2])
                    wt_cnt[('sc', u % 2)] += 64
                    wtreq[u] = (s_wtu[u % 2], wt_cnt[('sc', u % 2)])
                sc.wait_ge(s_pe, p9 + os_ * 64 + 32)
                if os_ >= 1:
                    sc.wait_ge(s_dve, d9 + os_ * 4)
                for oc4 in range(4):
                    sc.activation(ylb[:, oc4, 0:512], pb[2 * oc4][:, :], AF.Copy)
                    sc.activation(ylb[:, oc4, 512:1024], pb[2 * oc4 + 1][:, :],
                                  AF.Copy).then_inc(s_act, 2)
                sc.wait_ge(s_pe, p9 + os_ * 64 + 64)
                for oc4 in range(4):
                    sc.activation(yhb[:, oc4, 0:512], pb[2 * oc4][:, :], AF.Copy)
                    sc.activation(yhb[:, oc4, 512:1024], pb[2 * oc4 + 1][:, :],
                                  AF.Copy).then_inc(s_act, 2)
        n['act'] += 8 * 16

        @block.gpsimd
        def _(g):
            for os_ in range(8):
                for r in range(2, 4):
                    u = os_ * 4 + r
                    ih = r - 2
                    if u <= 3:
                        g.wait_ge(s_cch, 4)
                    if u >= 2:
                        g.wait_ge(s_pe, unit_mm_done(u - 2))
                    for kq in range(4):
                        g.dma_start(
                            out=wtf[:, u % 2, :, 128 * kq:128 * (kq + 1)],
                            in_=wh_full[kq, C * os_ + 2048 * ih:C * os_ + 2048 * (ih + 1),
                                        :].rearrange("(p q) n -> p q n", p=128)
                            ).then_inc(s_wtw[u % 2], 16)
                        dinc(s_wtw[u % 2])
                    wt_cnt[('g', u % 2)] += 64
                    wtreq[u] = (s_wtw[u % 2], wt_cnt[('g', u % 2)])

        @block.tensor
        def _(t):
            t.wait_ge(s_dve, d9)
            for os_ in range(8):
                for table in range(2):
                    for ih in range(2):
                        u = os_ * 4 + table * 2 + ih
                        t.wait_ge(*wtreq[u])
                        if table == 0 and ih == 0 and os_ > 0:
                            t.wait_ge(s_act, a9 + os_ * 16)
                        if table == 1 and ih == 0:
                            t.wait_ge(s_act, a9 + os_ * 16 + 8)
                        for icq in range(16):
                            ic = ih * 16 + icq
                            st_, sp_ = (ic == 0), (ic == NFC - 1)
                            for oc4 in range(4):
                                t.matmul(pb[2 * oc4][:, :],
                                         wtf[:, u % 2, icq, 128 * oc4:128 * (oc4 + 1)],
                                         xln[:, ic, 0:512], start=st_, stop=sp_)
                                li = t.matmul(pb[2 * oc4 + 1][:, :],
                                              wtf[:, u % 2, icq, 128 * oc4:128 * (oc4 + 1)],
                                              xln[:, ic, 512:1024], start=st_, stop=sp_)
                            li.then_inc(s_pe, 1)
        n['pe'] += 8 * 64

        @block.vector
        def _(v):
            v.wait_ge(s_bc, BCFIN)
            for os_ in range(8):
                v.wait_ge(s_act, a9 + os_ * 16 + 16)
                for oc4 in range(4):
                    oc = os_ * 4 + oc4
                    j = os_ * 4 + oc4
                    v.wait_ge(s_f9[j % 4], x9v[j])
                    v.tensor_sub(dsc[:], yhb[:, oc4, :], ylb[:, oc4, :])
                    v.tensor_mul(dmc[:], dsc[:], mkbc[:])
                    v.tensor_scalar_add(tmpf[:], ylb[:, oc4, :], fbb[:, oc:oc + 1])
                    v.tensor_add(tmpf[:], tmpf[:], x2f9[:, oc4, :])
                    v.tensor_add(x2f9[:, oc4, :], tmpf[:], dmc[:]).then_inc(s_dve, 1)
        n['dve'] += 32
        nc.leave_named_scope("P9_ffn", _p9scope[0], False)

    return nc


def kernel(L0_attn_x, L0_attn_Wqkv, L0_attn_bqkv, L0_attn_Wo, L0_attn_bo,
           L0_ffn_qweight, L0_ffn_lut4, L0_ffn_lut8, L0_ffn_bias):
    import ml_dtypes
    nc = build_program()
    ident = np.eye(128, dtype=np.float32)
    identb = np.eye(128, dtype=ml_dtypes.bfloat16)
    onec = np.ones((128, 1), dtype=ml_dtypes.bfloat16)
    oner = np.ones((1, 128), dtype=np.float32)
    bqkv_c = np.ascontiguousarray(np.asarray(L0_attn_bqkv, np.float32).reshape(24, 128).T)
    bo_c = np.ascontiguousarray(np.asarray(L0_attn_bo, np.float32).reshape(32, 128).T)
    fb_c = np.ascontiguousarray(np.asarray(L0_ffn_bias, np.float32).reshape(32, 128).T)
    x = np.asarray(L0_attn_x, np.float32)
    qw = np.asarray(L0_ffn_qweight, np.int32)
    wq = np.ascontiguousarray(np.asarray(L0_attn_Wqkv, np.float32))
    wof = np.ascontiguousarray(np.asarray(L0_attn_Wo, np.float32))
    l4 = np.asarray(L0_ffn_lut4, np.float32)
    l8 = np.asarray(L0_ffn_lut8, np.float32)
    in_maps = []
    for c in range(8):
        g, q = divmod(c, 4)
        in_maps.append({
            "xT": np.ascontiguousarray(x[g, 1024 * q:1024 * (q + 1), :].T),
            "wqkv": wq,
            "bqkv_c": bqkv_c,
            "wo": wof,
            "bo_c": bo_c,
            "qw_s": np.ascontiguousarray(qw[:, 512 * c:512 * (c + 1), :]),
            "l4_s": np.ascontiguousarray(l4[512 * c:512 * (c + 1), :]),
            "l8_s": np.ascontiguousarray(l8[512 * c:512 * (c + 1), :]),
            "fb_c": fb_c,
            "identf": ident,
            "identb": identb,
            "onecol_b": onec,
            "onerow_f": oner,
        })
    res = run_bass_kernel_spmd(nc, in_maps, core_ids=list(range(8)))
    out = np.empty((2, 4096, 4096), np.float32)
    for c in range(8):
        g, q = divmod(c, 4)
        out[g, 1024 * q:1024 * (q + 1), :] = res.results[c]["xout"].T
    return out
